# version 20
# baseline (speedup 1.0000x reference)
import numpy as np
import ml_dtypes
import concourse.bass as bass
import concourse.mybir as mybir
from concourse.bass_utils import run_bass_kernel_spmd

F32 = mybir.dt.float32
BF16 = mybir.dt.bfloat16
U8 = mybir.dt.uint8
AF = mybir.ActivationFunctionType
ALU = mybir.AluOpType
AXX = mybir.AxisListType.X

B, N, D = 8, 2048, 1024
HN, HD, FF = 8, 64, 4096
NT, KD, KF = 16, 8, 32
TQ, NTQ, TPQ = 512, 4, 4
ETA, DT_, GFLOOR, EPS = 0.1, 0.01, 0.1, 1e-5
KB = 1024
NW = float(FF * D)

SMALLS = {}

ENG2SEM = {"sync": "SQ", "gpsimd": "GQ", "tensor": "TE", "vector": "VE",
           "scalar": "SE"}
COMPUTE_SEMS = {"tensor": "TE", "vector": "VE", "scalar": "SE"}


def _consts():
    c = {}
    c["c_ident"] = np.eye(128, dtype=np.float32)
    c["c_ident_bf"] = np.eye(128, dtype=ml_dtypes.bfloat16)
    ipad = np.zeros((128, 256), dtype=np.float32)
    for k in range(128):
        ipad[k, 64 + k] = 1.0
    c["c_ipad"] = ipad
    c["c_cums"] = np.triu(np.ones((128, 128), dtype=np.float32))
    c["c_masku"] = (ETA * np.triu(np.ones((128, 128)))).astype(ml_dtypes.bfloat16)
    c["c_masku4"] = np.tile((ETA * np.triu(np.ones((128, 128)))), (1, 4)).astype(ml_dtypes.bfloat16)
    idm = np.zeros((128, 16), dtype=np.float32)
    idm[:, 0] = 1.0
    idm[:, 6] = 1.0
    idm[:, 8] = 1.0
    idm[:, 14] = 1.0
    c["c_idmat"] = idm
    edges = np.zeros((1, 16 * 128), dtype=np.float32)
    for r in range(8):
        s = 1 << r
        edges[0, r * 128:r * 128 + s] = 1.0
        edges[0, (8 + r) * 128 + 128 - s:(8 + r) * 128 + 128] = 1.0
    c["c_edges"] = edges
    c["c_ones_row"] = np.ones((1, 128), dtype=np.float32)
    c["c_ones_row_bf"] = np.ones((1, 128), dtype=ml_dtypes.bfloat16)
    c["c_ones_2048_bf"] = np.ones((1, 2048), dtype=ml_dtypes.bfloat16)
    return c


class MK:
    ENGS = ("sync", "gpsimd", "tensor", "vector", "scalar")
    NDMA = 8

    def __init__(self, nc):
        self.nc = nc
        self.ops = {e: [] for e in self.ENGS}
        self.nops = {e: 0 for e in self.ENGS}
        self.sems = {}
        self.cnt = {}
        self.pending = {e: {} for e in self.ENGS}
        self.drained = {e: -1 for e in self.ENGS}
        self.dma_idx = {"SQ": 0, "GQ": 0, "AQ": 0}

    def sem(self, name, handle):
        self.sems[name] = handle
        self.cnt[name] = 0

    def _flush(self, eng):
        pw = self.pending[eng]
        for key in sorted(pw, key=str):
            self.ops[eng].append({"wait": (key, pw[key])})
        pw.clear()

    def wait(self, eng, key, val):
        if val is None or val < 0 or (isinstance(key, str) and key in
                                      ("SQ", "GQ", "AQ")):
            return
        if key == eng:
            if eng != "tensor" and self.drained[eng] < val:
                self.drain(eng)
            return
        if key in self.ENGS:
            self.ops[key][val]["inc"] = True
        pw = self.pending[eng]
        if pw.get(key, -1) < val:
            pw[key] = val

    def raw_wait(self, eng, semname, val):
        self._flush(eng)
        self.ops[eng].append({"wait": (semname, val)})

    DMAQ = {"sync": "SQ", "gpsimd": "GQ", "scalar": "AQ"}

    def op(self, eng, fn, inc=None, dma=False):
        semname = ENG2SEM[eng]
        if dma and eng == "scalar":
            semname = "AQ"
        if semname in ("SQ", "GQ", "AQ"):
            idx = self.dma_idx[semname]
            self.dma_idx[semname] = (idx + 1) % self.NDMA
            sname = f"{semname}{idx}"
            prev = self.cnt[sname]
            if prev > 0:
                pw = self.pending[eng]
                if pw.get(sname, -1) < prev:
                    pw[sname] = prev
            self._flush(eng)
            self.cnt[sname] = prev + 16
            h = self.sems[sname]
            self.ops[eng].append({"fn": (lambda e, fn=fn, h=h:
                                         fn(e).then_inc(h, 16)), "inc": None})
            return (sname, prev + 16)
        self._flush(eng)
        rec = {"fn": fn, "inc": False}
        self.ops[eng].append(rec)
        i = self.nops[eng]
        self.nops[eng] += 1
        rec["opidx"] = len(self.ops[eng]) - 1
        return (eng, len(self.ops[eng]) - 1)

    def drain(self, eng):
        self._flush(eng)
        self.ops[eng].append({"drain": True})
        self.drained[eng] = len(self.ops[eng]) - 1

    def barrier(self):
        last = {}
        for eng in ("tensor", "vector", "scalar"):
            for i in range(len(self.ops[eng]) - 1, -1, -1):
                rec = self.ops[eng][i]
                if "fn" in rec and rec.get("inc") is not None:
                    last[eng] = i
                    break
        for eng in self.ENGS:
            for src_e, idx in last.items():
                if src_e == eng:
                    if self.drained[eng] < idx:
                        self.drain(eng)
                    continue
                self.wait(eng, src_e, idx)
            for s, v in self.cnt.items():
                if v > 0:
                    pw = self.pending[eng]
                    if pw.get(s, -1) < v:
                        pw[s] = v
            self._flush(eng)

    def run_block(self, block):
        for eng in self.ENGS:
            self._flush(eng)
        prefix = {}
        for eng in self.ENGS:
            pc, acc = [], 0
            for rec in self.ops[eng]:
                if rec.get("inc"):
                    acc += 1
                pc.append(acc)
            prefix[eng] = pc
        for eng in self.ENGS:
            ops = self.ops[eng]
            sems = self.sems
            pfx = prefix

            def body(e, ops=ops, sems=sems, pfx=pfx, eng=eng):
                for rec in ops:
                    if "drain" in rec:
                        e.drain()
                    elif "wait" in rec:
                        key, val = rec["wait"]
                        if key in self.ENGS:
                            v = pfx[key][val]
                            if v > 0:
                                e.wait_ge(sems[ENG2SEM[key]], v)
                        else:
                            e.wait_ge(sems[key], val)
                    elif rec.get("inc"):
                        h = sems[ENG2SEM[eng]]
                        rec["fn"](e).then_inc(h, 1)
                    else:
                        rec["fn"](e)

            getattr(block, eng)(body)


class Buf:
    def __init__(self, mk, name):
        self.mk = mk
        self.name = name
        self.w = None
        self.r = {}

    def note_read(self, fence):
        s, v = fence
        if self.r.get(s, 0) < v:
            self.r[s] = v


def build(nc, debug=False, upto="full"):
    mk = MK(nc)

    def din(name, shape):
        return nc.dram_tensor(name, shape, F32, kind="ExternalInput").ap()

    I = {}
    for k, nm, sh in [
        ("x0", "L0_attn_x", [N, D]),
        ("ln1_g", "L0_attn_ln1_g", [D]), ("ln1_b", "L0_attn_ln1_b", [D]),
        ("ln2_g", "L0_attn_ln2_g", [D]), ("ln2_b", "L0_attn_ln2_b", [D]),
        ("ln3_g", "L0_attn_ln3_g", [D]), ("ln3_b", "L0_attn_ln3_b", [D]),
        ("w_pot", "L0_attn_w_pot", [D, 2]), ("b_pot", "L0_attn_b_pot", [2]),
        ("w_bk", "L0_attn_w_bk", [2, D]), ("b_bk", "L0_attn_b_bk", [D]),
        ("w_qkv", "L0_attn_w_qkv", [D, 1536]), ("b_qkv", "L0_attn_b_qkv", [1536]),
        ("w_out", "L0_attn_w_out", [512, D]), ("b_out", "L0_attn_b_out", [D]),
        ("w_ff1", "L0_attn_w_ff1", [D, FF]), ("b_ff1", "L0_attn_b_ff1", [FF]),
        ("w_ff2", "L0_attn_w_ff2", [FF, D]), ("b_ff2", "L0_attn_b_ff2", [D]),
        ("fW1", "L0_ffn_W1", [FF, D]), ("fb1", "L0_ffn_b1", [FF]),
        ("fW2", "L0_ffn_W2", [D, FF]), ("fb2", "L0_ffn_b2", [D]),
    ]:
        I[k] = din(nm, sh)

    CN = _consts()
    C = {}
    for k, v in CN.items():
        dt = BF16 if v.dtype == ml_dtypes.bfloat16 else F32
        C[k] = nc.dram_tensor(k, list(v.shape), dt, kind="ExternalInput").ap()

    y = nc.dram_tensor("y", [N, D], F32, kind="ExternalOutput").ap()
    w1n = nc.dram_tensor("w1n", [FF, D], BF16, kind="Internal").ap()
    w2n = nc.dram_tensor("w2n", [D, FF], BF16, kind="Internal").ap()
    wff1n = nc.dram_tensor("wff1n", [D, FF], BF16, kind="Internal").ap()
    wff2n = nc.dram_tensor("wff2n", [FF, D], BF16, kind="Internal").ap()
    ypart = nc.dram_tensor("ypart", [N, D], F32, kind="Internal").ap()
    rsc = nc.dram_tensor("rowscratch", [8, N], F32, kind="Internal").ap()
    dbg = nc.dram_tensor("dbg", [128, 512], F32, kind="Internal").ap() if debug else None
    dbgxa = nc.dram_tensor("dbgxa", [128, 16384], F32, kind="Internal").ap() if debug else None
    dbgq = nc.dram_tensor("dbgq", [128, 6144], F32, kind="Internal").ap() if debug else None
    dbgh = nc.dram_tensor("dbgh", [128, 2048], F32, kind="Internal").ap() if debug else None

    import contextlib
    with contextlib.ExitStack() as _stack:
        _ec = _stack.enter_context
        arena = _ec(nc.sbuf_tensor("arena", [128, 185 * KB], U8))
        sbc = _ec(nc.sbuf_tensor("sbc", [128, 3712], U8))
        sbs = _ec(nc.sbuf_tensor("sbs", [128, 8192], U8))
        sbr = _ec(nc.sbuf_tensor("sbr", [128, 8192], U8))
        ps0 = _ec(nc.psum_tensor("ps0", [128, 512], F32))
        ps1 = _ec(nc.psum_tensor("ps1", [128, 512], F32))
        ps2 = _ec(nc.psum_tensor("ps2", [128, 512], F32))
        ps3 = _ec(nc.psum_tensor("ps3", [128, 512], F32))
        ps4 = _ec(nc.psum_tensor("ps4", [128, 512], F32))
        ps5 = _ec(nc.psum_tensor("ps5", [128, 512], F32))
        ps6 = _ec(nc.psum_tensor("ps6", [128, 512], F32))
        ps7 = _ec(nc.psum_tensor("ps7", [128, 512], F32))
        sem_te = _ec(nc.semaphore())
        sem_ve = _ec(nc.semaphore())
        sem_se = _ec(nc.semaphore())
        dma_sems = {}
        for qn in ("SQ", "GQ", "AQ"):
            for i in range(MK.NDMA):
                dma_sems[f"{qn}{i}"] = _ec(nc.semaphore(name=f"{qn}{i}"))
        block = _ec(nc.Block())
        PS = [ps0, ps1, ps2, ps3, ps4, ps5, ps6, ps7]
        for nm, h in [("TE", sem_te), ("VE", sem_ve), ("SE", sem_se)]:
            mk.sem(nm, h)
        for nm, h in dma_sems.items():
            mk.sem(nm, h)

        bufs = {}

        def T(name):
            if name not in bufs:
                bufs[name] = Buf(mk, name)
            return bufs[name]

        PB = [T(f"ps{i}") for i in range(8)]

        def run(eng, fn, reads=(), writes=(), inc=True, dma=False):
            for b in writes:
                if b.w is not None:
                    mk.wait(eng, b.w[0], b.w[1])
                for s, v in b.r.items():
                    mk.wait(eng, s, v)
            for b in reads:
                if b.w is not None:
                    mk.wait(eng, b.w[0], b.w[1])
            f = mk.op(eng, fn, inc=inc, dma=dma)
            for b in writes:
                b.w = f
                b.r = {}
            for b in reads:
                b.note_read(f)
            return f

        def v8(t, off, dt, cols, p0=0, npts=128):
            esz = 4 if dt == F32 else (2 if dt == BF16 else 1)
            return t[p0:p0 + npts, off:off + cols * esz].bitcast(dt)

        acc = v8(arena, 0, F32, 16 * KB)
        wq_res = v8(arena, 0, BF16, 12 * KB)
        wo_res = v8(arena, 24 * KB, BF16, 4 * KB)
        hebqs = [v8(arena, 32 * KB, BF16, 1024), v8(arena, 56 * KB, BF16, 1024)]
        hebts = [v8(arena, 34 * KB, BF16, 1024), v8(arena, 58 * KB, BF16, 1024)]
        hebAs = [v8(arena, 36 * KB, BF16, 1024), v8(arena, 60 * KB, BF16, 1024)]
        hebW = v8(arena, 38 * KB, F32, 256)
        hebWb = v8(arena, 39 * KB, BF16, 512)
        hebCs = [v8(arena, 40 * KB, F32, 1), v8(arena, 62 * KB, F32, 1)]
        hebo = v8(arena, 41 * KB, BF16, 2048)
        xn2bf = v8(arena, 48 * KB, BF16, 4096)
        xn3bf = v8(arena, 0, BF16, 4096)
        f1slabs = [v8(arena, 16 * KB, BF16, 4096), v8(arena, 24 * KB, BF16, 4096)]
        wf2slab = v8(arena, 32 * KB, BF16, 4096)
        yrd = v8(arena, 40 * KB, F32, 4096)

        xa = v8(arena, 64 * KB, F32, 16 * KB)
        xnbf = v8(arena, 64 * KB, BF16, 16 * KB)
        wstage = v8(arena, 96 * KB, F32, 2048)
        cs1 = v8(arena, 96 * KB, BF16, 4096)
        tstg = [v8(arena, 96 * KB, F32, 2048), v8(arena, 128 * KB, F32, 2048),
                v8(arena, 136 * KB, F32, 2048), v8(arena, 144 * KB, F32, 2048)]
        tbsl = [v8(arena, 104 * KB, BF16, 2048), v8(arena, 152 * KB, BF16, 2048)]
        ttn = [v8(arena, 108 * KB, BF16, 2048), v8(arena, 156 * KB, BF16, 2048)]
        wtA = v8(arena, 104 * KB, BF16, 1024)
        wtB = v8(arena, 106 * KB, BF16, 1024)
        wtern = v8(arena, 108 * KB, BF16, 2048)
        fgslab = v8(arena, 112 * KB, BF16, 8192)

        h_bf = v8(arena, 128 * KB, BF16, 16 * KB)
        h2_bf = h_bf
        qkvbf = v8(arena, 128 * KB, BF16, 12 * KB)
        vrows_tmp = v8(arena, 128 * KB, F32, 4096)
        w2slab = v8(arena, 160 * KB, BF16, 2048)
        junk = v8(arena, 164 * KB, F32, 1024)
        stg = v8(arena, 168 * KB, F32, 2048)
        out_tm = v8(arena, 176 * KB, F32, 2048)

        co = [0]

        def cvv(dt, cols):
            esz = 4 if dt == F32 else 2
            off = co[0]
            co[0] += cols * esz
            assert co[0] <= 3712
            return sbc[:, off:off + cols * esz].bitcast(dt)

        ident = cvv(F32, 128)
        ident_bf = cvv(BF16, 128)
        ipad = cvv(F32, 256)
        cums_c = cvv(F32, 128)
        masku = cvv(BF16, 128)
        masku4 = cvv(BF16, 512)
        idmat = cvv(F32, 16)
        pass

        so = [0]

        def sv(dt, cols, npts=128, p0=0, nm=None):
            esz = 4 if dt == F32 else 2
            off = so[0]
            so[0] += cols * esz
            assert so[0] <= 8192, so[0]
            if nm is not None:
                SMALLS[nm] = (off, cols, "f32" if dt == F32 else "bf16", p0, npts)
            return sbs[p0:p0 + npts, off:off + cols * esz].bitcast(dt)

        m0 = sv(F32, 16, nm="m0"); r0 = sv(F32, 16, nm="r0"); v0 = sv(F32, 16, nm="v0")
        m2 = sv(F32, 16, nm="m2"); r2 = sv(F32, 16, nm="r2")
        m3 = sv(F32, 16, nm="m3"); r3 = sv(F32, 16, nm="r3")
        mr0 = sv(F32, 16); mr2 = sv(F32, 16); mr3 = sv(F32, 16)
        r1 = sv(F32, 16, nm="r1")
        ssum = sv(F32, 16, nm="ssum"); ssq = sv(F32, 16, nm="ssq")
        st_a = sv(F32, 16); st_b = sv(F32, 16)
        epscol = sv(F32, 1); onecol = sv(F32, 1); onescolf = sv(F32, 1)
        tcols = sv(F32, 4, nm="tcols")
        csums = sv(F32, 64)
        g1c = sv(F32, 8); b1c = sv(F32, 8)
        g2c = sv(F32, 8); b2c = sv(F32, 8)
        g3c = sv(F32, 8); b3c = sv(F32, 8)
        bqkvc = sv(F32, 12)
        bff1c = sv(F32, 32)
        fb1c = sv(F32, 32)
        wpotc = sv(BF16, 16)
        wpotf = sv(F32, 16)
        cpotc = sv(F32, 2, nm="cpotc")
        tiny = sv(F32, 8)
        potV = sv(F32, 16, nm="potV"); potP = sv(F32, 16, nm="potP"); gam = sv(F32, 16, nm="gam")
        vbk = sv(F32, 16, nm="vbk"); gbk = sv(F32, 16, nm="gbk")
        s1 = {nm: sv(F32, 36, nm="s1" + nm) for nm in ("fre", "fim", "bre", "bim")}
        pk_a = sv(F32, 16, nm="pk_a"); pk_b = sv(F32, 16, nm="pk_b")
        vfc = sv(F32, 8, nm="vfc")
        vbc = sv(F32, 8, nm="vbc")
        ST = [sv(F32, 16) for _ in range(14)]
        grebk = sv(F32, 16, nm="grebk"); gimbk = sv(F32, 16, nm="gimbk")
        chat = sv(F32, 16, nm="chat"); rchat = sv(F32, 16, nm="rchat"); ratio = sv(F32, 16, nm="ratio")
        row16 = sv(F32, 256, npts=16)
        srow = sv(F32, 4, npts=1)

        def rvw(p0, npts, off, dt, cols):
            esz = 4 if dt == F32 else 2
            return sbr[p0:p0 + npts, off:off + cols * esz].bitcast(dt)

        G3 = rvw(0, 3, 0, BF16, 2048)
        wbk3 = rvw(0, 3, 4096, BF16, 1024)
        chr_ = rvw(0, 1, 6144, F32, 128)
        rchr = rvw(0, 1, 6656, F32, 128)
        cpr = rvw(0, 1, 7168, F32, 2)
        cendr = rvw(0, 1, 7176, F32, 16)
        potb2 = rvw(0, 2, 7240, F32, 1)
        ones_row = rvw(0, 1, 7680, F32, 128)
        b2row = rvw(32, 1, 0, BF16, 1024)
        boutrow = rvw(32, 1, 2048, BF16, 1024)
        bff2row = rvw(32, 1, 4096, BF16, 1024)
        ones_row_bf = rvw(32, 1, 6144, BF16, 128)
        edges_c = rvw(64, 1, 0, F32, 2048)
        idrow64 = rvw(64, 1, 8192 - 64, F32, 16)

        xnbf3 = xnbf.rearrange("p (k t) -> p k t", k=KD)
        xn2bf3 = xn2bf.rearrange("p (k t) -> p k t", k=KD)
        xn3bf3 = xn3bf.rearrange("p (k t) -> p k t", k=KD)
        fgslab3 = fgslab.rearrange("p (k f) -> p k f", k=KD)
        f1slab3s = [t.rearrange("p (k f) -> p k f", k=KD) for t in f1slabs]

        def acc_t(tt):
            return acc[:, tt * 1024:(tt + 1) * 1024]

        def xa_t(tt):
            return xa[:, tt * 1024:(tt + 1) * 1024]

        def h_tile(f):
            return h_bf[:, f * 512:(f + 1) * 512]

        def qkv_tile(slot, f):
            return qkvbf[:, slot * 6144 + f * 512:slot * 6144 + (f + 1) * 512]

        ACCB = T("acc")

        src_x = I["x0"].rearrange("(g p) d -> p g d", p=128)
        acc3 = acc.rearrange("p (g d) -> p g d", g=NT)
        for ap_, key in [(ident, "c_ident"), (ident_bf, "c_ident_bf"),
                         (ipad, "c_ipad"), (cums_c, "c_cums"),
                         (masku, "c_masku"), (masku4, "c_masku4"),
                         (idmat, "c_idmat")]:
            run("sync", lambda e, a=ap_, k=key: e.dma_start(
                a[0:C[k].shape[0], 0:C[k].shape[1]], C[k][:]),
                writes=[T("consts")])
        run("sync", lambda e: e.dma_start(ones_row, C["c_ones_row"][:]),
            writes=[T("orow")])
        run("sync", lambda e: e.dma_start(edges_c, C["c_edges"][:]),
            writes=[T("consts")])
        run("sync", lambda e: e.dma_start(idrow64, C["c_idmat"][0:1, :]),
            writes=[T("consts")])
        run("sync", lambda e: e.dma_start(ones_row_bf, C["c_ones_row_bf"][:]),
            writes=[T("orow")])
        run("gpsimd", lambda e: e.dma_start(b2row, I["fb2"][None, :]), writes=[T("b2row")])
        run("gpsimd", lambda e: e.dma_start(boutrow, I["b_out"][None, :]), writes=[T("boutrow")])
        run("gpsimd", lambda e: e.dma_start(bff2row, I["b_ff2"][None, :]), writes=[T("bff2row")])
        run("gpsimd", lambda e: e.dma_start(wbk3[0:2, :], I["w_bk"][:, :]), writes=[T("wbk3")])
        run("gpsimd", lambda e: e.dma_start(wbk3[2:3, :], I["b_bk"][None, :]), writes=[T("wbk3")])
        run("sync", lambda e: e.dma_start(potb2, I["b_pot"][:, None]), writes=[T("potb2")])
        for k in range(KD):
            run("sync", lambda e, k=k: e.dma_start(
                wpotf[:, 2 * k:2 * k + 2], I["w_pot"][k * 128:(k + 1) * 128, :]),
                writes=[T("wpotf")])
        for i, nm in enumerate(["ln1_g", "ln1_b", "ln2_g", "ln2_b", "ln3_g", "ln3_b"]):
            run("sync", lambda e, i=i, nm=nm: e.dma_start(
                vrows_tmp[i:i + 1, 0:1024], I[nm][None, :]), writes=[T("vrt")])
        run("sync", lambda e: e.dma_start(vrows_tmp[32:33, :], I["b_ff1"][None, :]),
            writes=[T("vrt")])
        run("sync", lambda e: e.dma_start(vrows_tmp[33:34, :], I["fb1"][None, :]),
            writes=[T("vrt")])
        run("sync", lambda e: e.dma_start(vrows_tmp[64:65, 0:1536], I["b_qkv"][None, :]),
            writes=[T("vrt")])
        for half in range(2):
            run("sync" if half == 0 else "gpsimd", lambda e, hh=half: e.dma_start(
                acc3[:, hh * 8:(hh + 1) * 8, :],
                src_x[:, hh * 8:(hh + 1) * 8, :]), writes=[ACCB])
        run("vector", lambda e: e.memset(epscol[:], EPS), writes=[T("epscol")])
        run("vector", lambda e: e.memset(onecol[:], 1.0), writes=[T("onecol")])
        run("vector", lambda e: e.memset(onescolf[:], 1.0), writes=[T("onescolf")])

        for k in range(KD):
            run("tensor", lambda e, k=k: e.transpose(
                PS[6][:, k * 6:(k + 1) * 6],
                vrows_tmp[0:6, k * 128:(k + 1) * 128], ident[0:6, 0:6]),
                reads=[T("vrt"), T("consts")], writes=[PB[6]])
        for i, col in enumerate([g1c, b1c, g2c, b2c, g3c, b3c]):
            run("vector", lambda e, c=col, i=i: e.tensor_copy(
                c[:, 0:8], PS[6][:, i:i + 43:6]),
                reads=[PB[6]], writes=[T("lncols")])
        for half in range(2):
            for kk in range(16):
                k = half * 16 + kk
                run("tensor", lambda e, k=k, kk=kk: e.transpose(
                    PS[7][:, kk * 2:kk * 2 + 2],
                    vrows_tmp[32:34, k * 128:(k + 1) * 128], ident[32:34, 32:34]),
                    reads=[T("vrt")], writes=[PB[7]])
            run("vector", lambda e, half=half: e.tensor_copy(
                bff1c[:, half * 16:(half + 1) * 16], PS[7][:, 0:31:2]),
                reads=[PB[7]], writes=[T("ffcols")])
            run("vector", lambda e, half=half: e.tensor_copy(
                fb1c[:, half * 16:(half + 1) * 16], PS[7][:, 1:32:2]),
                reads=[PB[7]], writes=[T("ffcols")])
        for k in range(12):
            run("tensor", lambda e, k=k: e.transpose(
                PS[6][:, 64 + k:65 + k],
                vrows_tmp[64:65, k * 128:(k + 1) * 128], ident[64:65, 64:65]),
                reads=[T("vrt")], writes=[PB[6]])
        run("vector", lambda e: e.tensor_copy(bqkvc[:], PS[6][:, 64:76]),
            reads=[PB[6]], writes=[T("bqkvc")])
        for k in range(KD):
            run("vector", lambda e, k=k: e.tensor_scalar(
                wpotc[:, 2 * k:2 * k + 2], wpotf[:, 2 * k:2 * k + 2],
                g1c[:, k:k + 1], None, ALU.mult),
                reads=[T("wpotf"), T("lncols")], writes=[T("wpotc")])

        for k in range(KD):
            run("tensor", lambda e, k=k: e.matmul(
                PS[5][0:2, 0:1], wpotf[:, 2 * k:2 * k + 2], b1c[:, k:k + 1],
                start=(k == 0), stop=(k == KD - 1)),
                reads=[T("wpotf"), T("lncols")], writes=[PB[5]])
        run("vector", lambda e: e.tensor_add(tiny[0:2, 0:1], PS[5][0:2, 0:1], potb2),
            reads=[PB[5], T("potb2")], writes=[T("tinyA")])
        run("tensor", lambda e: e.transpose(PS[5][0:1, 4:6], tiny[0:2, 0:1],
                                            ident[0:2, 0:2]),
            reads=[T("tinyA"), T("consts")], writes=[PB[5]])
        run("vector", lambda e: e.tensor_copy(cpr[:], PS[5][0:1, 4:6]),
            reads=[PB[5]], writes=[T("cpr")])
        run("tensor", lambda e: e.matmul(PS[5][:, 8:10], ones_row, cpr[:],
                                         start=True, stop=True),
            reads=[T("cpr"), T("orow")], writes=[PB[5]])
        run("vector", lambda e: e.tensor_copy(cpotc[:], PS[5][:, 8:10]),
            reads=[PB[5]], writes=[T("cpotc")])
        run("vector", lambda e: e.tensor_scalar(
            cpotc[:, 0:1], cpotc[:, 0:1], 2.0, None, ALU.add),
            reads=[], writes=[T("cpotc")])

        def w_tile_src(i):
            if i < 32:
                return I["fW1"][i * 128:(i + 1) * 128, :]
            j = i - 32
            return I["fW2"][(j // 4) * 128:(j // 4 + 1) * 128,
                            (j % 4) * 1024:(j % 4 + 1) * 1024]

        stg_b = [T(f"wstage{s}") for s in range(4)]
        bsl_b = [T("tbsl0"), T("tbsl1")]
        tern_b = [T("wtern0"), T("wtern1")]
        vb_ = T("vrt")
        _hand = stg_b[1:] + [bsl_b[1], tern_b[1]]
        for bx in _hand:
            if vb_.w is not None:
                bx.w = vb_.w
            bx.r = dict(vb_.r)

        def wd_src(di):
            if di < 16:
                return I["fW1"][di * 256:(di + 1) * 256, :].rearrange(
                    "(a p) d -> p a d", p=128)
            j = (di - 16) * 2
            r, q = j // 4, j % 4
            return I["fW2"][r * 128:(r + 1) * 128, q * 1024:(q + 2) * 1024]

        def wd_dst(di):
            if di < 16:
                return (w1n[di * 256:(di + 1) * 256, :].rearrange(
                    "(a p) d -> p a d", p=128), [T(f"w12n_w1_{di // 4}")])
            j = (di - 16) * 2
            r, q = j // 4, j % 4
            return (w2n[r * 128:(r + 1) * 128, q * 1024:(q + 2) * 1024],
                    [T(f"w12n_w2_{qq * 8 + ff}") for qq in (q, q + 1)
                     for ff in range(8)])

        def tern_scale(wi):
            lo = wi * 16
            run("vector", lambda e, wi=wi, lo=lo: e.reduce_sum(
                tiny[:, 2 + wi:3 + wi], csums[:, lo:lo + 16], axis=AXX),
                reads=[T("csums")], writes=[T("tinyB")])
            run("tensor", lambda e, wi=wi: e.matmul(
                PS[5][0:1, 12 + wi:13 + wi], onescolf, tiny[:, 2 + wi:3 + wi],
                start=True, stop=True),
                reads=[T("tinyB"), T("onescolf")], writes=[PB[5]])
            run("vector", lambda e, wi=wi: e.tensor_scalar(
                srow[:, wi:wi + 1], PS[5][0:1, 12 + wi:13 + wi],
                1.0 / NW, 1e-8, ALU.mult, ALU.max),
                reads=[PB[5]], writes=[T("srow")])
            run("vector", lambda e, wi=wi: e.tensor_scalar(
                srow[:, 2 + wi:3 + wi], srow[:, wi:wi + 1], 0.5, None,
                ALU.mult), reads=[], writes=[T("srow")])
            run("tensor", lambda e, wi=wi: e.matmul(
                PS[5][:, 16 + wi:17 + wi], ones_row, srow[:, 2 + wi:3 + wi],
                start=True, stop=True),
                reads=[T("srow"), T("orow")], writes=[PB[5]])
            run("vector", lambda e, wi=wi: e.tensor_copy(
                tcols[:, wi:wi + 1], PS[5][:, 16 + wi:17 + wi]),
                reads=[PB[5]], writes=[T("tcols")])
            run("vector", lambda e, wi=wi: e.tensor_scalar(
                tcols[:, 2 + wi:3 + wi], tcols[:, wi:wi + 1], -1.0, None,
                ALU.mult), reads=[], writes=[T("tcols")])

        def tern_p1(di, eng):
            sl = di % 4
            wv = tstg[sl]
            run(eng, lambda e, w=wv, di=di: e.dma_start(w[:], wd_src(di)),
                writes=[stg_b[sl]], dma=True)
            run("scalar", lambda e, w=wv, di=di: e.activation(
                w[:], w[:], AF.Abs, accum_out=csums[:, di:di + 1]),
                reads=[],
                writes=[stg_b[sl]] + ([T("csums")] if di in (15, 31) else []))

        def tern_p2(di, eng):
            sl = di % 4
            wv = tstg[sl]
            tv, tb = ttn[di % 2], tern_b[di % 2]
            bv, bb_ = tbsl[di % 2], bsl_b[di % 2]
            wi = 0 if di < 16 else 1
            run(eng, lambda e, w=wv, di=di: e.dma_start(w[:], wd_src(di)),
                writes=[stg_b[sl]], dma=True)
            run("vector", lambda e, w=wv, o=tv, wi=wi: e.tensor_scalar(
                o[:], w[:], tcols[:, wi:wi + 1], None, ALU.is_ge),
                reads=[stg_b[sl], T("tcols")], writes=[tb])
            run("vector", lambda e, w=wv, o=bv, wi=wi: e.tensor_scalar(
                o[:], w[:], tcols[:, 2 + wi:3 + wi], None, ALU.is_le),
                reads=[stg_b[sl], T("tcols")], writes=[bb_])
            run("vector", lambda e, t=tv, b2_=bv: e.tensor_sub(
                t[:], t[:], b2_[:]),
                reads=[tb, bb_], writes=[tb])
            dst, wrbufs = wd_dst(di)
            run("gpsimd", lambda e, d=dst, t=tv, di=di: e.dma_start(
                d, t[:] if di >= 16 else t.rearrange("p (a d) -> p a d", a=2)),
                reads=[tb], writes=wrbufs)


        def tern_rest():
            for di in range(16):
                tern_p2(di, "sync")
            for di in range(16, 32):
                tern_p1(di, "sync")
            tern_scale(1)

        def tern_pass2_w2():
            for di in range(16, 32):
                tern_p2(di, "gpsimd")
            for bx in _hand + [stg_b[0], bsl_b[0], tern_b[0]]:
                hb_ = T("h")
                if bx.w is not None:
                    hb_.note_read(bx.w)
                for sname_, v_ in bx.r.items():
                    if hb_.r.get(sname_, 0) < v_:
                        hb_.r[sname_] = v_

        for di in range(16):
            tern_p1(di, "sync")

        for tt in range(NT):
            run("vector", lambda e, tt=tt: e.reduce_sum(
                ssum[:, tt:tt + 1], acc_t(tt), axis=AXX),
                reads=[ACCB], writes=[T("ssum")])
            run("scalar", lambda e, tt=tt: e.activation(
                junk[:], acc_t(tt), AF.Square, accum_out=ssq[:, tt:tt + 1]),
                reads=[ACCB], writes=[T("ssq"), T("junk")])
        run("vector", lambda e: e.tensor_scalar(m0[:], ssum[:], 1.0 / D, None,
                                                ALU.mult),
            reads=[T("ssum")], writes=[T("m0")])
        run("vector", lambda e: e.tensor_tensor(st_a[:], m0[:], m0[:], ALU.mult),
            reads=[T("m0")], writes=[T("sta")])
        run("vector", lambda e: e.scalar_tensor_tensor(
            v0[:], ssq[:], 1.0 / D, st_a[:], ALU.mult, ALU.subtract),
            reads=[T("ssq"), T("sta")], writes=[T("v0")])
        run("scalar", lambda e: e.activation(st_b[:], v0[:], AF.Ln,
                                             bias=epscol[:], scale=1.0),
            reads=[T("v0"), T("epscol")], writes=[T("stb")])
        run("scalar", lambda e: e.activation(r0[:], st_b[:], AF.Exp, scale=-0.5),
            reads=[T("stb")], writes=[T("r0")])
        run("vector", lambda e: e.tensor_scalar(st_a[:], v0[:], EPS, None, ALU.add),
            reads=[T("v0")], writes=[T("sta")])
        run("vector", lambda e: e.reciprocal(st_b[:], st_a[:]),
            reads=[T("sta")], writes=[T("stb")])
        run("vector", lambda e: e.tensor_tensor(st_a[:], v0[:], st_b[:], ALU.mult),
            reads=[T("v0"), T("stb")], writes=[T("sta")])
        run("scalar", lambda e: e.activation(st_b[:], st_a[:], AF.Ln,
                                             bias=epscol[:], scale=1.0),
            reads=[T("sta"), T("epscol")], writes=[T("stb")])
        run("scalar", lambda e: e.activation(r1[:], st_b[:], AF.Exp, scale=-0.5),
            reads=[T("stb")], writes=[T("r1")])
        run("vector", lambda e: e.tensor_tensor(mr0[:], m0[:], r0[:], ALU.mult),
            reads=[T("m0"), T("r0")], writes=[T("mr0")])
        run("vector", lambda e: e.tensor_scalar(mr0[:], mr0[:], -1.0, None,
                                                ALU.mult),
            reads=[], writes=[T("mr0")])

        tern_scale(0)
        tern_rest()
        tern_pass2_w2()

        def ln_apply_tq(tq, mrcol, rcol, dst3, mbuf, rbuf, src_t, srcbuf, dstbuf,
                        gcol=None, bcol=None):
            sb_of = ((lambda tt: srcbuf[tt]) if isinstance(srcbuf, list)
                     else (lambda tt: srcbuf))
            for lt_ in range(TPQ):
                tt = tq * TPQ + lt_
                sl = tt % 2
                srot = stg[:, sl * 1024:(sl + 1) * 1024]
                sb = T(f"stg{sl}")
                run("scalar", lambda e, s=srot, tt=tt, mr=mrcol, r=rcol,
                    src_t=src_t: e.activation(
                        s, src_t(tt), AF.Identity, scale=r[:, tt:tt + 1],
                        bias=mr[:, tt:tt + 1]),
                    reads=[sb_of(tt), mbuf, rbuf], writes=[sb])
                pb = PB[2] if sl == 0 else PB[3]
                pst = PS[2] if sl == 0 else PS[3]
                for k in range(KD):
                    run("tensor", lambda e, k=k, pst=pst, s=srot: e.transpose(
                        pst[:, (k % 4) * 128:(k % 4) * 128 + 128],
                        s[:, k * 128:(k + 1) * 128], ident),
                        reads=[sb, T("consts")], writes=[pb])
                    if k in (3, 7):
                        base = k - 3
                        if gcol is None:
                            run("scalar", lambda e, b=base, pst=pst, lt_=lt_,
                                tq=tq: e.activation(
                                    dst3[:, b:b + 4,
                                         tq * 512 + lt_ * 128:tq * 512 + (lt_ + 1) * 128]
                                    if dst3 is xnbf3 else
                                    dst3[:, b:b + 4, lt_ * 128:(lt_ + 1) * 128],
                                    pst[:, 0:512].rearrange("p (k t) -> p k t", k=4),
                                    AF.Identity),
                                reads=[pb], writes=[dstbuf])
                        else:
                            for kk in range(base, base + 4):
                                run("vector", lambda e, kk=kk, pst=pst, lt_=lt_:
                                    e.tensor_scalar(
                                        dst3[:, kk, lt_ * 128:(lt_ + 1) * 128],
                                        pst[:, (kk % 4) * 128:(kk % 4) * 128 + 128],
                                        gcol[:, kk:kk + 1], bcol[:, kk:kk + 1],
                                        ALU.mult, ALU.add),
                                    reads=[pb, T("lncols")], writes=[dstbuf])

        for tq in range(NTQ):
            ln_apply_tq(tq, mr0, r0, xnbf3, T("mr0"), T("r0"), acc_t, ACCB,
                        T("xnbf"))
            for lt_ in range(TPQ):
                tt = tq * TPQ + lt_
                for k in range(KD):
                    run("tensor", lambda e, k=k, tt=tt: e.matmul(
                        PS[4][:, 2 * tt:2 * tt + 2],
                        xnbf3[:, k, tt * 128:(tt + 1) * 128],
                        wpotc[:, 2 * k:2 * k + 2],
                        start=(k == 0), stop=(k == KD - 1)),
                        reads=[T("xnbf"), T("wpotc")], writes=[PB[4]])

        run("vector", lambda e: e.tensor_tensor(potV[:], PS[4][:, 0:32:2], r1[:],
                                                ALU.mult),
            reads=[PB[4], T("r1")], writes=[T("potV")])
        run("vector", lambda e: e.tensor_tensor(potP[:], PS[4][:, 1:32:2], r1[:],
                                                ALU.mult),
            reads=[PB[4], T("r1")], writes=[T("potP")])
        run("vector", lambda e: e.tensor_scalar(potV[:], potV[:], cpotc[:, 0:1],
                                                None, ALU.add),
            reads=[T("cpotc")], writes=[T("potV")])
        run("vector", lambda e: e.tensor_scalar(potP[:], potP[:], cpotc[:, 1:2],
                                                None, ALU.add),
            reads=[T("cpotc")], writes=[T("potP")])
        run("scalar", lambda e: e.activation(st_a[:], potP[:], AF.Abs),
            reads=[T("potP")], writes=[T("sta")])
        run("scalar", lambda e: e.activation(st_a[:], st_a[:], AF.Exp, scale=-1.0),
            reads=[], writes=[T("sta")])
        run("scalar", lambda e: e.activation(st_a[:], st_a[:], AF.Ln,
                                             bias=onecol[:], scale=1.0),
            reads=[T("onecol")], writes=[T("sta")])
        run("scalar", lambda e: e.activation(st_b[:], potP[:], AF.Relu),
            reads=[T("potP")], writes=[T("stb")])
        run("vector", lambda e: e.tensor_add(gam[:], st_a[:], st_b[:]),
            reads=[T("sta"), T("stb")], writes=[T("gam")])
        run("vector", lambda e: e.tensor_scalar(gam[:], gam[:], GFLOOR, None,
                                                ALU.add),
            reads=[], writes=[T("gam")])

        run("tensor", lambda e: e.transpose(PS[5][0:16, 0:128], potV[:], ident),
            reads=[T("potV"), T("consts")], writes=[PB[5]])
        run("tensor", lambda e: e.transpose(PS[5][0:16, 128:256], gam[:], ident),
            reads=[T("gam"), T("consts")], writes=[PB[5]])
        run("vector", lambda e: e.tensor_copy(row16[:, 0:256], PS[5][0:16, 0:256]),
            reads=[PB[5]], writes=[T("row16")])
        run("gpsimd", lambda e: e.dma_start(
            rsc[0:1, :].rearrange("o (p j) -> (o p) j", p=16), row16[:, 0:128]),
            reads=[T("row16")], writes=[T("rsc0")], dma=True)
        run("gpsimd", lambda e: e.dma_start(
            rsc[1:2, :].rearrange("o (p j) -> (o p) j", p=16), row16[:, 128:256]),
            reads=[T("row16")], writes=[T("rsc1")], dma=True)
        run("gpsimd", lambda e: e.dma_start(
            vbk[:], rsc[0:1, :].rearrange("o (p j) -> (o p) j", p=128)),
            reads=[T("rsc0")], writes=[T("vbk")], dma=True)
        run("gpsimd", lambda e: e.dma_start(
            gbk[:], rsc[1:2, :].rearrange("o (p j) -> (o p) j", p=128)),
            reads=[T("rsc1")], writes=[T("gbk")], dma=True)

        for nm in ("fre", "fim", "bre", "bim"):
            run("vector", lambda e, t=s1[nm]: e.memset(t[:, 0:4], 0.0),
                writes=[T("s1" + nm)])
        run("vector", lambda e: e.memset(s1["fre"][:, 1:2], 1.0), writes=[T("s1fre")])
        run("vector", lambda e: e.memset(s1["fre"][:, 2:3], 1.0), writes=[T("s1fre")])
        run("vector", lambda e: e.memset(s1["bre"][:, 1:2], 1.0), writes=[T("s1bre")])
        run("vector", lambda e: e.memset(s1["bre"][:, 2:3], 1.0), writes=[T("s1bre")])

        def bk_step(j):
            for tb in ("f", "b"):
                re_ = s1["fre" if tb == "f" else "bre"]
                im_ = s1["fim" if tb == "f" else "bim"]
                rb_ = T("s1" + ("fre" if tb == "f" else "bre"))
                ib_ = T("s1" + ("fim" if tb == "f" else "bim"))
                dj = j if tb == "f" else 15 - j
                prev = slice(2 * j + 2, 2 * j + 4)
                pprev = slice(2 * j, 2 * j + 2)
                cur = slice(2 * j + 4, 2 * j + 6)
                t0 = ST[0] if tb == "f" else ST[1]
                t1 = ST[2] if tb == "f" else ST[3]
                tb0 = T("bkt0" + tb)
                tb1 = T("bkt1" + tb)
                run("vector", lambda e, re_=re_, im_=im_, dj=dj, t0=t0, prev=prev,
                    pprev=pprev: e.scalar_tensor_tensor(
                        t0[:, 0:2], im_[:, prev], gbk[:, dj:dj + 1],
                        re_[:, pprev], ALU.mult, ALU.subtract),
                    reads=[rb_, ib_, T("gbk")], writes=[tb0])
                run("vector", lambda e, re_=re_, im_=im_, dj=dj, t1=t1, prev=prev,
                    pprev=pprev: e.scalar_tensor_tensor(
                        t1[:, 0:2], re_[:, prev], gbk[:, dj:dj + 1],
                        im_[:, pprev], ALU.mult, ALU.add),
                    reads=[rb_, ib_, T("gbk")], writes=[tb1])
                run("vector", lambda e, re_=re_, dj=dj, t0=t0, prev=prev, cur=cur:
                    e.scalar_tensor_tensor(
                        re_[:, cur], re_[:, prev], vbk[:, dj:dj + 1], t0[:, 0:2],
                        ALU.mult, ALU.add),
                    reads=[tb0, T("vbk")], writes=[rb_])
                run("vector", lambda e, im_=im_, dj=dj, t1=t1, prev=prev, cur=cur:
                    e.scalar_tensor_tensor(
                        im_[:, cur], im_[:, prev], vbk[:, dj:dj + 1], t1[:, 0:2],
                        ALU.mult, ALU.subtract),
                    reads=[tb1, T("vbk")], writes=[ib_])

        def bk_renorm(j):
            for tb in ("f", "b"):
                re_ = s1["fre" if tb == "f" else "bre"]
                im_ = s1["fim" if tb == "f" else "bim"]
                rb_ = T("s1" + ("fre" if tb == "f" else "bre"))
                ib_ = T("s1" + ("fim" if tb == "f" else "bim"))
                nb = T("nrm")
                lo = 2 * j + 2
                hi = 2 * j + 6
                run("vector", lambda e, re_=re_, lo=lo: e.tensor_reduce(
                    ST[4][:, 0:1], re_[:, lo:lo + 4], AXX, ALU.max,
                    apply_absolute_value=True), reads=[rb_], writes=[nb])
                run("vector", lambda e, im_=im_, lo=lo: e.tensor_reduce(
                    ST[4][:, 1:2], im_[:, lo:lo + 4], AXX, ALU.max,
                    apply_absolute_value=True), reads=[ib_], writes=[nb])
                run("vector", lambda e: e.tensor_max(ST[4][:, 2:3], ST[4][:, 0:1],
                                                     ST[4][:, 1:2]),
                    reads=[nb], writes=[nb])
                run("vector", lambda e: e.reciprocal(ST[4][:, 3:4], ST[4][:, 2:3]),
                    reads=[nb], writes=[nb])
                for t_, b_ in ((re_, rb_), (im_, ib_)):
                    run("vector", lambda e, t_=t_, hi=hi: e.tensor_scalar(
                        t_[:, 0:hi], t_[:, 0:hi], ST[4][:, 3:4], None,
                        ALU.mult), reads=[nb], writes=[b_])

        for j in range(16):
            bk_step(j)
            if j in (4, 9):
                bk_renorm(j)

        def pack_from_s1(dst, dstb):
            for (tb, base) in (("f", 0), ("b", 8)):
                re_ = s1["fre" if tb == "f" else "bre"]
                im_ = s1["fim" if tb == "f" else "bim"]
                rb_ = T("s1" + ("fre" if tb == "f" else "bre"))
                ib_ = T("s1" + ("fim" if tb == "f" else "bim"))
                run("vector", lambda e, d=dst, b=base, re_=re_: e.tensor_copy(
                    d[:, b:b + 4:2], re_[:, 34:36]), reads=[rb_], writes=[dstb])
                run("vector", lambda e, d=dst, b=base, im_=im_: e.tensor_copy(
                    d[:, b + 1:b + 4:2], im_[:, 34:36]), reads=[ib_], writes=[dstb])
                run("vector", lambda e, d=dst, b=base, re_=re_: e.tensor_copy(
                    d[:, b + 4:b + 8:2], re_[:, 32:34]), reads=[rb_], writes=[dstb])
                run("vector", lambda e, d=dst, b=base, im_=im_: e.tensor_copy(
                    d[:, b + 5:b + 8:2], im_[:, 32:34]), reads=[ib_], writes=[dstb])

        pack_from_s1(pk_a, T("pka"))


        def s2pair(t, col):
            return t[:, col:col + 9:8]

        def psum_pair(colf):
            return (PS[5][:, 32 + colf:33 + colf], PS[6][:, colf:colf + 1])

        def stage2_round(s, src, srcb, dst, dstb, last=False):
            r_ = s.bit_length() - 1
            run("tensor", lambda e, s=s, src=src: e.matmul(
                PS[5][:, 32:40], ipad[:, 64 - s:192 - s], src[:, 0:8],
                start=True, stop=False),
                reads=[srcb, T("consts")], writes=[PB[5]])
            run("tensor", lambda e, r_=r_: e.matmul(
                PS[5][:, 32:40], edges_c[:, r_ * 128:(r_ + 1) * 128],
                idrow64[:, 0:8], start=False, stop=True),
                reads=[T("consts")], writes=[PB[5]])
            run("tensor", lambda e, s=s, src=src: e.matmul(
                PS[6][:, 0:8], ipad[:, 64 + s:192 + s], src[:, 8:16],
                start=True, stop=False),
                reads=[srcb, T("consts")], writes=[PB[6]])
            run("tensor", lambda e, r_=r_: e.matmul(
                PS[6][:, 0:8], edges_c[:, (8 + r_) * 128:(9 + r_) * 128],
                idrow64[:, 8:16], start=False, stop=True),
                reads=[T("consts")], writes=[PB[6]])
            if last:
                run("vector", lambda e: e.tensor_copy(vfc[:, 0:2], PS[5][:, 32:34]),
                    reads=[PB[5]], writes=[T("vfc")])
                run("vector", lambda e: e.tensor_copy(vfc[:, 2:4], PS[5][:, 36:38]),
                    reads=[PB[5]], writes=[T("vfc")])
                run("vector", lambda e: e.tensor_copy(vbc[:, 0:2], PS[6][:, 0:2]),
                    reads=[PB[6]], writes=[T("vbc")])
                run("vector", lambda e: e.tensor_copy(vbc[:, 2:4], PS[6][:, 4:6]),
                    reads=[PB[6]], writes=[T("vbc")])
                return

            def emit_dir(goff_t, gslc, base):
                FA, FB, FC, FD = base + 0, base + 2, base + 4, base + 6
                pairs = [
                    (base + 0, (FA, 0, FB, 4)),
                    (base + 2, (FA, 2, FB, 6)),
                    (base + 4, (FC, 0, FD, 4)),
                    (base + 6, (FC, 2, FD, 6)),
                ]
                for (dc, (f1, gg1, f2, gg2)) in pairs:
                    run("vector", lambda e, f1=f1, g=gg1: e.tensor_tensor(
                        ST[5][:, 0:1], src[:, f1:f1 + 1], gslc(g), ALU.mult),
                        reads=[srcb, goff_t], writes=[T("stm")])
                    run("vector", lambda e, f1=f1, g=gg1: e.tensor_tensor(
                        ST[5][:, 1:2], src[:, f1 + 1:f1 + 2], gslc(g + 1), ALU.mult),
                        reads=[srcb, goff_t], writes=[T("stm")])
                    run("vector", lambda e, f2=f2, g=gg2: e.tensor_tensor(
                        ST[5][:, 2:3], src[:, f2:f2 + 1], gslc(g), ALU.mult),
                        reads=[srcb, goff_t], writes=[T("stm")])
                    run("vector", lambda e, f2=f2, g=gg2: e.tensor_tensor(
                        ST[5][:, 3:4], src[:, f2 + 1:f2 + 2], gslc(g + 1), ALU.mult),
                        reads=[srcb, goff_t], writes=[T("stm")])
                    run("vector", lambda e: e.tensor_sub(ST[6][:, 0:1],
                                                         ST[5][:, 0:1], ST[5][:, 1:2]),
                        reads=[T("stm")], writes=[T("stm2")])
                    run("vector", lambda e: e.tensor_sub(ST[6][:, 1:2],
                                                         ST[5][:, 2:3], ST[5][:, 3:4]),
                        reads=[T("stm")], writes=[T("stm2")])
                    run("vector", lambda e, dc=dc: e.tensor_add(
                        dst[:, dc:dc + 1], ST[6][:, 0:1], ST[6][:, 1:2]),
                        reads=[T("stm2")], writes=[dstb])
                    run("vector", lambda e, f1=f1, g=gg1: e.tensor_tensor(
                        ST[5][:, 0:1], src[:, f1:f1 + 1], gslc(g + 1), ALU.mult),
                        reads=[srcb, goff_t], writes=[T("stm")])
                    run("vector", lambda e, f1=f1, g=gg1: e.tensor_tensor(
                        ST[5][:, 1:2], src[:, f1 + 1:f1 + 2], gslc(g), ALU.mult),
                        reads=[srcb, goff_t], writes=[T("stm")])
                    run("vector", lambda e, f2=f2, g=gg2: e.tensor_tensor(
                        ST[5][:, 2:3], src[:, f2:f2 + 1], gslc(g + 1), ALU.mult),
                        reads=[srcb, goff_t], writes=[T("stm")])
                    run("vector", lambda e, f2=f2, g=gg2: e.tensor_tensor(
                        ST[5][:, 3:4], src[:, f2 + 1:f2 + 2], gslc(g), ALU.mult),
                        reads=[srcb, goff_t], writes=[T("stm")])
                    run("vector", lambda e: e.tensor_add(ST[6][:, 0:1],
                                                         ST[5][:, 0:1], ST[5][:, 1:2]),
                        reads=[T("stm")], writes=[T("stm2")])
                    run("vector", lambda e: e.tensor_add(ST[6][:, 1:2],
                                                         ST[5][:, 2:3], ST[5][:, 3:4]),
                        reads=[T("stm")], writes=[T("stm2")])
                    run("vector", lambda e, dc=dc: e.tensor_add(
                        dst[:, dc + 1:dc + 2], ST[6][:, 0:1], ST[6][:, 1:2]),
                        reads=[T("stm2")], writes=[dstb])

            emit_dir(PB[5], lambda g: PS[5][:, 32 + g:33 + g], 0)
            emit_dir(PB[6], lambda g: PS[6][:, g:g + 1], 8)
            for base in (0, 8):
                run("vector", lambda e, d=dst, b=base: e.tensor_reduce(
                    ST[4][:, 0:1], d[:, b:b + 8], AXX, ALU.max,
                    apply_absolute_value=True),
                    reads=[dstb], writes=[T("nrm")])
                run("vector", lambda e: e.reciprocal(ST[4][:, 1:2], ST[4][:, 0:1]),
                    reads=[T("nrm")], writes=[T("nrm")])
                run("vector", lambda e, d=dst, b=base: e.tensor_scalar(
                    d[:, b:b + 8], d[:, b:b + 8], ST[4][:, 1:2], None, ALU.mult),
                    reads=[T("nrm")], writes=[dstb])

        _bk2 = {"cur": pk_a, "curb": T("pka"), "oth": pk_b, "othb": T("pkb"),
                "done": 0}

        def emit_bk_round():
            i = _bk2["done"]
            if i > 7:
                return
            if i < 7:
                stage2_round(1 << i, _bk2["cur"], _bk2["curb"], _bk2["oth"],
                             _bk2["othb"])
                _bk2["cur"], _bk2["curb"], _bk2["oth"], _bk2["othb"] = (
                    _bk2["oth"], _bk2["othb"], _bk2["cur"], _bk2["curb"])
            else:
                stage2_round(1, _bk2["cur"], _bk2["curb"], _bk2["oth"],
                             _bk2["othb"], last=True)
            _bk2["done"] = i + 1

        fgs_b = [T(f"fgs{k}") for k in range(KD)]
        hb = T("h")
        w2s_b = [T("w2s0"), T("w2s1")]
        otm_b = [T("otm0"), T("otm1")]
        for tq in range(NTQ):
            for fg in range(4):
                for k in range(KD):
                    run("sync", lambda e, k=k, fg=fg: e.dma_start(
                        fgslab3[:, k, :],
                        w1n[fg * 1024:(fg + 1) * 1024, k * 128:(k + 1) * 128],
                        transpose=True),
                        reads=[T(f"w12n_w1_{fg}")], writes=[fgs_b[k]], dma=True)
                for fl in range(8):
                    f = fg * 8 + fl
                    pb = PB[f % 2]
                    pst = PS[f % 2]
                    for k in range(KD):
                        run("tensor", lambda e, k=k, fl=fl, tq=tq, pst=pst: e.matmul(
                            pst[:, 0:512],
                            fgslab3[:, k, fl * 128:(fl + 1) * 128],
                            xnbf3[:, k, tq * 512:(tq + 1) * 512],
                            start=(k == 0), stop=(k == KD - 1)),
                            reads=[fgs_b[k], T("xnbf")], writes=[pb])
                    run("scalar", lambda e, f=f, pst=pst: e.activation(
                        h_tile(f), pst[:, 0:512], AF.Gelu, bias=fb1c[:, f:f + 1]),
                        reads=[pb, T("ffcols")], writes=[hb])
                emit_bk_round()
            for f in range(KF):
                sl = f % 2
                wv = w2slab[:, sl * 1024:(sl + 1) * 1024]
                run("sync", lambda e, w=wv, f=f: e.dma_start(
                    w, w2n[0:1024, f * 128:(f + 1) * 128], transpose=True),
                    reads=[T(f"w12n_w2_{f}")], writes=[w2s_b[sl]], dma=True)
                for lt_ in range(TPQ):
                    for dh in range(2):
                        pi = lt_ * 2 + dh
                        run("tensor", lambda e, f=f, lt_=lt_, dh=dh, pi=pi, w=wv:
                            e.matmul(PS[pi][:, 0:512],
                                     h_tile(f)[:, lt_ * 128:(lt_ + 1) * 128],
                                     w[:, dh * 512:(dh + 1) * 512],
                                     start=(f == 0), stop=False),
                            reads=[w2s_b[sl], hb], writes=[PB[pi]])
            for lt_ in range(TPQ):
                for dh in range(2):
                    pi = lt_ * 2 + dh
                    run("tensor", lambda e, pi=pi, dh=dh: e.matmul(
                        PS[pi][:, 0:512], ones_row_bf,
                        b2row[:, dh * 512:(dh + 1) * 512],
                        start=False, stop=True),
                        reads=[T("b2row"), T("orow")], writes=[PB[pi]])
            for lt_ in range(TPQ):
                tt = tq * TPQ + lt_
                osl = tt % 2
                ov = out_tm[:, osl * 1024:(osl + 1) * 1024]
                for dh in range(2):
                    pi = lt_ * 2 + dh
                    run("vector", lambda e, pi=pi, dh=dh, o=ov, tt=tt: e.tensor_add(
                        o[:, dh * 512:(dh + 1) * 512], PS[pi][:, 0:512],
                        acc_t(tt)[:, dh * 512:(dh + 1) * 512]),
                        reads=[PB[pi], ACCB], writes=[otm_b[osl]])
                run("scalar", lambda e, o=ov, tt=tt: e.dma_start(
                    ypart[tt * 128:(tt + 1) * 128, :], o),
                    reads=[otm_b[osl]], writes=[T("ypart")], dma=True)
            for c1 in (tq * 2, tq * 2 + 1):
                run("gpsimd", lambda e, c1=c1: e.dma_start(
                    cs1[:], I["w_ff1"][c1 * 128:(c1 + 1) * 128, :]),
                    writes=[T("cs1"), stg_b[0], stg_b[1]], dma=True)
                run("gpsimd", lambda e, c1=c1: e.dma_start(
                    wff1n[c1 * 128:(c1 + 1) * 128, :], cs1[:]),
                    reads=[T("cs1")], writes=[T("wff1n")], dma=True)
            for c2 in range(tq * 8, tq * 8 + 8):
                wsb = T("wtA") if c2 % 2 == 0 else T("wtB")
                wsv = wtA if c2 % 2 == 0 else wtB
                run("gpsimd", lambda e, c2=c2, w=wsv: e.dma_start(
                    w[:], I["w_ff2"][c2 * 128:(c2 + 1) * 128, :]),
                    writes=[wsb], dma=True)
                run("gpsimd", lambda e, c2=c2, w=wsv: e.dma_start(
                    wff2n[c2 * 128:(c2 + 1) * 128, :], w[:]),
                    reads=[wsb], writes=[T("wff2n")], dma=True)

        while _bk2["done"] <= 7:
            emit_bk_round()


        run("vector", lambda e: e.tensor_scalar(vfc[:, 4:8], vfc[:, 0:4], -1.0,
                                                None, ALU.mult),
            reads=[], writes=[T("vfc")])
        run("vector", lambda e: e.tensor_scalar(vbc[:, 4:8], vbc[:, 0:4], -1.0,
                                                None, ALU.mult),
            reads=[], writes=[T("vbc")])

        def stage3_dir(tb, vv, vvb, numre, numim, denre, denim, ob):
            re_ = s1["fre" if tb == "f" else "bre"]
            im_ = s1["fim" if tb == "f" else "bim"]
            rb_ = T("s1" + ("fre" if tb == "f" else "bre"))
            ib_ = T("s1" + ("fim" if tb == "f" else "bim"))
            if tb == "f":
                Are, Bre = re_[:, 4:36:2], re_[:, 5:36:2]
                Aim, Bim = im_[:, 4:36:2], im_[:, 5:36:2]
                Cre, Dre = re_[:, 2:34:2], re_[:, 3:34:2]
                Cim, Dim = im_[:, 2:34:2], im_[:, 3:34:2]
            else:
                Are, Bre = re_[:, 34:2:-2], re_[:, 35:3:-2]
                Aim, Bim = im_[:, 34:2:-2], im_[:, 35:3:-2]
                Cre, Dre = re_[:, 32:0:-2], re_[:, 33:1:-2]
                Cim, Dim = im_[:, 32:0:-2], im_[:, 33:1:-2]
            specs = [
                (numre, (Are, Aim, Bre, Bim), (0, 5, 2, 7)),
                (numim, (Are, Aim, Bre, Bim), (1, 0, 3, 2)),
                (denre, (Cre, Cim, Dre, Dim), (0, 5, 2, 7)),
                (denim, (Cre, Cim, Dre, Dim), (1, 0, 3, 2)),
            ]
            for (out_, P, Q) in specs:
                run("vector", lambda e, o=out_, P=P, Q=Q, vv=vv: e.tensor_scalar(
                    o[:], P[0], vv[:, Q[0]:Q[0] + 1], None, ALU.mult),
                    reads=[rb_, ib_, vvb], writes=[ob])
                for idx in (1, 2, 3):
                    run("vector", lambda e, o=out_, P=P, Q=Q, idx=idx, vv=vv:
                        e.scalar_tensor_tensor(
                            o[:], P[idx], vv[:, Q[idx]:Q[idx] + 1], o[:],
                            ALU.mult, ALU.add),
                        reads=[rb_, ib_, vvb], writes=[ob])

        nf_re, nf_im, df_re, df_im = ST[0], ST[1], ST[2], ST[3]
        nb_re, nb_im, db_re, db_im = ST[5], ST[6], ST[7], ST[8]
        stage3_dir("f", vfc, T("vfc"), nf_re, nf_im, df_re, df_im, T("s3f"))
        stage3_dir("b", vbc, T("vbc"), nb_re, nb_im, db_re, db_im, T("s3b"))
        s3f, s3b = T("s3f"), T("s3b")
        wd_re, wd_im = ST[9], ST[10]
        t1_re, t1_im = ST[11], ST[12]
        TMP = ST[13]

        def vv_mult(o, a, b_, rd, wr):
            run("vector", lambda e, o=o, a=a, b_=b_: e.tensor_tensor(
                o[:], a[:], b_[:], ALU.mult), reads=rd, writes=[wr])

        def vv_addsub(o, a, b_, op, rd, wr):
            run("vector", lambda e, o=o, a=a, b_=b_, op=op: e.tensor_tensor(
                o[:], a[:], b_[:], op), reads=rd, writes=[wr])

        vv_mult(wd_re, df_re, db_re, [s3f, s3b], T("wd"))
        vv_mult(TMP, df_im, db_im, [s3f, s3b], T("cmt"))
        vv_addsub(wd_re, wd_re, TMP, ALU.subtract, [T("cmt"), T("wd")], T("wd"))
        vv_mult(wd_im, df_re, db_im, [s3f, s3b], T("wd"))
        vv_mult(TMP, df_im, db_re, [s3f, s3b], T("cmt"))
        vv_addsub(wd_im, wd_im, TMP, ALU.add, [T("cmt"), T("wd")], T("wd"))
        vv_mult(t1_re, nf_re, db_re, [s3f, s3b], T("t1"))
        vv_mult(TMP, nf_im, db_im, [s3f, s3b], T("cmt"))
        vv_addsub(t1_re, t1_re, TMP, ALU.subtract, [T("cmt"), T("t1")], T("t1"))
        vv_mult(t1_im, nf_re, db_im, [s3f, s3b], T("t1"))
        vv_mult(TMP, nf_im, db_re, [s3f, s3b], T("cmt"))
        vv_addsub(t1_im, t1_im, TMP, ALU.add, [T("cmt"), T("t1")], T("t1"))
        for (a, b_, dstc, op) in ((nb_re, df_re, t1_re, ALU.add),
                                  (nb_im, df_im, t1_re, ALU.subtract),
                                  (nb_re, df_im, t1_im, ALU.add),
                                  (nb_im, df_re, t1_im, ALU.add)):
            vv_mult(TMP, a, b_, [s3f, s3b], T("cmt"))
            vv_addsub(dstc, dstc, TMP, op, [T("cmt"), T("t1")], T("t1"))
        vv_mult(nf_re, vbk, wd_re, [T("vbk"), T("wd")], T("dw"))
        vv_mult(TMP, gbk, wd_im, [T("gbk"), T("wd")], T("cmt"))
        vv_addsub(nf_re, nf_re, TMP, ALU.add, [T("cmt"), T("dw")], T("dw"))
        vv_mult(nf_im, vbk, wd_im, [T("vbk"), T("wd")], T("dw"))
        vv_mult(TMP, gbk, wd_re, [T("gbk"), T("wd")], T("cmt"))
        vv_addsub(nf_im, nf_im, TMP, ALU.subtract, [T("cmt"), T("dw")], T("dw"))
        vv_addsub(t1_re, t1_re, nf_re, ALU.subtract, [T("dw"), T("t1")], T("t1"))
        vv_addsub(t1_im, t1_im, nf_im, ALU.subtract, [T("dw"), T("t1")], T("t1"))
        vv_mult(df_re, t1_re, t1_re, [T("t1")], T("mag"))
        vv_mult(TMP, t1_im, t1_im, [T("t1")], T("cmt"))
        vv_addsub(df_re, df_re, TMP, ALU.add, [T("cmt"), T("mag")], T("mag"))
        run("vector", lambda e: e.reciprocal(df_im[:], df_re[:]),
            reads=[T("mag")], writes=[T("rmag")])
        vv_mult(grebk, wd_re, t1_re, [T("wd"), T("t1")], T("grebk"))
        vv_mult(TMP, wd_im, t1_im, [T("wd"), T("t1")], T("cmt"))
        vv_addsub(grebk, grebk, TMP, ALU.add, [T("cmt"), T("grebk")], T("grebk"))
        vv_mult(grebk, grebk, df_im, [T("rmag"), T("grebk")], T("grebk"))
        vv_mult(gimbk, wd_im, t1_re, [T("wd"), T("t1")], T("gimbk"))
        vv_mult(TMP, wd_re, t1_im, [T("wd"), T("t1")], T("cmt"))
        vv_addsub(gimbk, gimbk, TMP, ALU.subtract, [T("cmt"), T("gimbk")], T("gimbk"))
        vv_mult(gimbk, gimbk, df_im, [T("rmag"), T("gimbk")], T("gimbk"))

        run("scalar", lambda e: e.dma_start(
            rsc[2:3, :].rearrange("o (p j) -> (o p) j", p=128), grebk[:]),
            reads=[T("grebk")], writes=[T("rsc2")], dma=True)
        run("scalar", lambda e: e.dma_start(
            rsc[3:4, :].rearrange("o (p j) -> (o p) j", p=128), gimbk[:]),
            reads=[T("gimbk")], writes=[T("rsc3")], dma=True)
        run("gpsimd", lambda e: e.dma_start(G3[0:1, :], rsc[2:3, :]),
            reads=[T("rsc2")], writes=[T("G3")])
        run("gpsimd", lambda e: e.dma_start(G3[1:2, :], rsc[3:4, :]),
            reads=[T("rsc3")], writes=[T("G3")])
        run("sync", lambda e: e.dma_start(G3[2:3, :], C["c_ones_2048_bf"][:]),
            writes=[T("G3")])

        def tap(col, t, b, n=16):
            if debug:
                run("gpsimd", lambda e, col=col, t=t, n=n: e.dma_start(
                    dbg[:, col:col + n], t[:, 0:n]), reads=[b], writes=[T("dbg")])

        if debug:
            tap(0, potV, T("potV"))
            tap(16, potP, T("potP"))
            tap(32, gam, T("gam"))
            tap(48, vbk, T("vbk"))
            tap(64, gbk, T("gbk"))
            tap(80, grebk, T("grebk"))
            tap(96, gimbk, T("gimbk"))
            tap(112, r1, T("r1"))
            tap(128, m0, T("m0"))
            tap(144, r0, T("r0"))

        def finish_early(src_kind):
            if src_kind == "ypart":
                for tt in range(NT):
                    run("sync", lambda e, tt=tt: e.dma_start(
                        y[tt * 128:(tt + 1) * 128, :],
                        ypart[tt * 128:(tt + 1) * 128, :]),
                        reads=[T("ypart")], writes=[T("y")])
            elif src_kind == "xa":
                for tt in range(NT):
                    run("sync", lambda e, tt=tt: e.dma_start(
                        y[tt * 128:(tt + 1) * 128, :], xa_t(tt)),
                        reads=[XT[tt]], writes=[T("y")])
            for i in range(MK.NDMA):
                for q, e2 in (("SQ", "sync"), ("GQ", "gpsimd"), ("AQ", "scalar")):
                    cntv = mk.cnt.get(f"{q}{i}", 0)
                    if cntv > 0:
                        mk.raw_wait("sync", f"{q}{i}", cntv)
                        if e2 != "sync":
                            mk.raw_wait(e2, f"{q}{i}", cntv)
            mk.run_block(block)

        if upto == "ffn":
            finish_early("ypart")
            return nc

        mk.barrier()
        XT = [T(f"xa{t}") for t in range(NT)]
        for tt in range(NT):
            run("tensor", lambda e, tt=tt: e.matmul(
                PS[2][:, 0:512], G3[:, tt * 128:(tt + 1) * 128], wbk3[:, 0:512],
                start=True, stop=True),
                reads=[T("G3"), T("wbk3")], writes=[PB[2]])
            run("tensor", lambda e, tt=tt: e.matmul(
                PS[3][:, 0:512], G3[:, tt * 128:(tt + 1) * 128], wbk3[:, 512:1024],
                start=True, stop=True),
                reads=[T("G3"), T("wbk3")], writes=[PB[3]])
            run("vector", lambda e, tt=tt: e.tensor_scalar(
                xa_t(tt), acc_t(tt), m0[:, tt:tt + 1], r0[:, tt:tt + 1],
                ALU.subtract, ALU.mult),
                reads=[ACCB, T("m0"), T("r0")], writes=[XT[tt]])
            run("vector", lambda e, tt=tt: e.tensor_add(
                xa_t(tt)[:, 0:512], xa_t(tt)[:, 0:512], PS[2][:, 0:512]),
                reads=[PB[2]], writes=[XT[tt]])
            run("vector", lambda e, tt=tt: e.tensor_add(
                xa_t(tt)[:, 512:1024], xa_t(tt)[:, 512:1024], PS[3][:, 0:512]),
                reads=[PB[3]], writes=[XT[tt]])

        def xa_stats(mcol, rcol, mb, rb, mrcol, mrb):
            for tt in range(NT):
                run("vector", lambda e, tt=tt: e.reduce_sum(
                    ssum[:, tt:tt + 1], xa_t(tt), axis=AXX),
                    reads=[XT[tt]], writes=[T("ssum")])
                run("scalar", lambda e, tt=tt: e.activation(
                    junk[:], xa_t(tt), AF.Square, accum_out=ssq[:, tt:tt + 1]),
                    reads=[XT[tt]], writes=[T("ssq"), T("junk")])
            run("vector", lambda e, m=mcol: e.tensor_scalar(
                m[:], ssum[:], 1.0 / D, None, ALU.mult),
                reads=[T("ssum")], writes=[mb])
            run("vector", lambda e, m=mcol: e.tensor_tensor(
                st_a[:], m[:], m[:], ALU.mult), reads=[mb], writes=[T("sta")])
            run("vector", lambda e: e.scalar_tensor_tensor(
                st_b[:], ssq[:], 1.0 / D, st_a[:], ALU.mult, ALU.subtract),
                reads=[T("ssq"), T("sta")], writes=[T("stb")])
            run("scalar", lambda e: e.activation(st_a[:], st_b[:], AF.Ln,
                                                 bias=epscol[:], scale=1.0),
                reads=[T("stb"), T("epscol")], writes=[T("sta")])
            run("scalar", lambda e, r=rcol: e.activation(r[:], st_a[:], AF.Exp,
                                                         scale=-0.5),
                reads=[T("sta")], writes=[rb])
            run("vector", lambda e, m=mcol, r=rcol, mr=mrcol: e.tensor_tensor(
                mr[:], m[:], r[:], ALU.mult), reads=[mb, rb], writes=[mrb])
            run("vector", lambda e, mr=mrcol: e.tensor_scalar(
                mr[:], mr[:], -1.0, None, ALU.mult), reads=[], writes=[mrb])

        xa_stats(m2, r2, T("m2"), T("r2"), mr2, T("mr2"))
        if debug:
            tap(160, m2, T("m2"))
            tap(176, r2, T("r2"))
            run("gpsimd", lambda e: e.dma_start(dbgxa[:, 0:16384], xa[:]),
                reads=XT, writes=[T("dbg")])

        if upto == "xa":
            finish_early("xa")
            return nc

        mk.barrier()
        run("gpsimd", lambda e: e.dma_start(
            wq_res[:, 0:1536], I["w_qkv"][0:128, :]),
            writes=[T("wqres"), ACCB])
        for k in range(1, KD):
            run("gpsimd", lambda e, k=k: e.dma_start(
                wq_res[:, k * 1536:(k + 1) * 1536],
                I["w_qkv"][k * 128:(k + 1) * 128, :]),
                writes=[T("wqres")])
        run("gpsimd", lambda e: e.dma_start(
            wo_res[:, 0:1024], I["w_out"][0:128, :]),
            writes=[T("wores"), ACCB])
        for k in range(1, 4):
            run("gpsimd", lambda e, k=k: e.dma_start(
                wo_res[:, k * 1024:(k + 1) * 1024],
                I["w_out"][k * 128:(k + 1) * 128, :]),
                writes=[T("wores")])
        run("tensor", lambda e: e.matmul(PS[5][:, 64:80], cums_c, gam[:],
                                         start=True, stop=True),
            reads=[T("gam"), T("consts")], writes=[PB[5]])
        run("scalar", lambda e: e.activation(chat[:], PS[5][:, 64:80], AF.Exp,
                                             scale=-DT_),
            reads=[PB[5]], writes=[T("chat")])
        run("vector", lambda e: e.reciprocal(rchat[:], chat[:]),
            reads=[T("chat")], writes=[T("rchat")])
        run("tensor", lambda e: e.transpose(PS[5][0:16, 80:208], chat[:], ident),
            reads=[T("chat"), T("consts")], writes=[PB[5]])
        run("vector", lambda e: e.tensor_copy(row16[:, 0:128], PS[5][0:16, 80:208]),
            reads=[PB[5]], writes=[T("row16")])
        run("tensor", lambda e: e.transpose(PS[5][0:1, 208:224],
                                            row16[:, 127:128], ident[0:16, 0:16]),
            reads=[T("row16"), T("consts")], writes=[PB[5]])
        run("vector", lambda e: e.tensor_copy(cendr[:], PS[5][0:1, 208:224]),
            reads=[PB[5]], writes=[T("cendr")])
        run("tensor", lambda e: e.matmul(PS[5][:, 224:240], ones_row, cendr[:],
                                         start=True, stop=True),
            reads=[T("cendr"), T("orow")], writes=[PB[5]])
        run("vector", lambda e: e.tensor_scalar(ratio[:], PS[5][:, 224:240],
                                                ETA, None, ALU.mult),
            reads=[PB[5]], writes=[T("ratio")])
        run("vector", lambda e: e.tensor_tensor(ratio[:], ratio[:], rchat[:],
                                                ALU.mult),
            reads=[T("rchat")], writes=[T("ratio")])
        run("vector", lambda e: e.memset(hebW[:], 0.0), writes=[T("hebW"), ACCB])
        run("vector", lambda e: e.memset(hebWb[:], 0.0), writes=[T("hebWb"), ACCB])

        for tq in range(NTQ):
            if upto == "qh_prep" :
                break
            slot = tq % 2
            qb = T(f"qkv{slot}")
            ln_apply_tq(tq, mr2, r2, xn2bf3, T("mr2"), T("r2"), xa_t, XT,
                        T("xn2bf"), gcol=g2c, bcol=b2c)
            if tq == 0:
                T("xn2bf").note_read((None, 0))
            for f in range(12):
                pb = PB[f % 2]
                pst = PS[f % 2]
                for k in range(KD):
                    run("tensor", lambda e, f=f, k=k, pst=pst: e.matmul(
                        pst[:, 0:512],
                        wq_res[:, k * 1536 + f * 128:k * 1536 + (f + 1) * 128],
                        xn2bf3[:, k, 0:512],
                        start=(k == 0), stop=(k == KD - 1)),
                        reads=[T("wqres"), T("xn2bf")], writes=[pb])
                run("scalar", lambda e, f=f, pst=pst, slot=slot: e.activation(
                    qkv_tile(slot, f), pst[:, 0:512], AF.Identity,
                    bias=bqkvc[:, f:f + 1]),
                    reads=[pb, T("bqkvc")], writes=[qb])
            if debug and tq == 0:
                run("gpsimd", lambda e: e.dma_start(dbgq[:, 0:6144],
                    qkvbf[:, 0:6144]), reads=[T("qkv0")], writes=[T("dbg")])
            if upto == "qh_qkv":
                continue
            def q_prep(c, slot=slot, qb=qb, tq=tq):
                Cg = tq * 4 + c
                par = c % 2
                hq, hqb = hebqs[par], T(f"hebq{par}")
                ht, htb = hebts[par], T(f"hebt{par}")
                hA, hAb = hebAs[par], T(f"hebA{par}")
                hC, hCb = hebCs[par], T(f"hebC{par}")
                pA = [PB[0], PB[1]]
                run("tensor", lambda e, Cg=Cg: e.transpose(
                    PS[5][0:1, 256:384], chat[:, Cg:Cg + 1], ident),
                    reads=[T("chat"), T("consts")], writes=[PB[5]])
                run("vector", lambda e: e.tensor_copy(chr_[:], PS[5][0:1, 256:384]),
                    reads=[PB[5]], writes=[T("chr")])
                run("tensor", lambda e, Cg=Cg: e.transpose(
                    PS[5][0:1, 384:512], rchat[:, Cg:Cg + 1], ident),
                    reads=[T("rchat"), T("consts")], writes=[PB[5]])
                run("vector", lambda e: e.tensor_copy(rchr[:], PS[5][0:1, 384:512]),
                    reads=[PB[5]], writes=[T("rchr")])
                run("tensor", lambda e: e.matmul(PS[5][:, 0:128], ones_row,
                                                 chr_[:], start=True, stop=True),
                    reads=[T("chr"), T("orow")], writes=[PB[5]])
                run("tensor", lambda e: e.matmul(PS[5][:, 128:256], ones_row,
                                                 rchr[:], start=True, stop=True),
                    reads=[T("rchr"), T("orow")], writes=[PB[5]])
                run("vector", lambda e, hC=hC: e.tensor_copy(
                    hC[:], PS[5][:, 127:128]),
                    reads=[PB[5]], writes=[hCb])
                qsrc0 = qkvbf[:]
                cbrep = bass.AP(PS[5][:].tensor, PS[5][:].offset,
                                [PS[5][:].ap[0], [0, 4], [1, 128]])
                rbrep = bass.AP(PS[5][:].tensor, PS[5][:].offset + 128,
                                [PS[5][:].ap[0], [0, 4], [1, 128]])
                run("vector", lambda e, slot=slot, c=c, cbrep=cbrep, hq=hq:
                    e.tensor_tensor(
                        hq.rearrange("p (r t) -> p r t", r=8)[:, 0:4, :],
                        bass.AP(qsrc0.tensor, qsrc0.offset + slot * 6144 + c * 128,
                                [qsrc0.ap[0], [512, 4], [1, 128]]),
                        cbrep, ALU.mult),
                    reads=[qb, PB[5]], writes=[hqb])
                run("vector", lambda e, slot=slot, c=c, rbrep=rbrep, hq=hq:
                    e.tensor_tensor(
                        hq.rearrange("p (r t) -> p r t", r=8)[:, 4:8, :],
                        bass.AP(qsrc0.tensor,
                                qsrc0.offset + slot * 6144 + 2048 + c * 128,
                                [qsrc0.ap[0], [512, 4], [1, 128]]),
                        rbrep, ALU.mult),
                    reads=[qb, PB[5]], writes=[hqb])
                ps6b = PS[6][:, 0:256].bitcast(BF16)
                ps7b = PS[7][:, 0:256].bitcast(BF16)
                for fc in range(4):
                    run("tensor", lambda e, fc=fc, slot=slot, c=c, p=ps6b:
                        e.transpose(
                            p[:, fc * 128:(fc + 1) * 128],
                            qkv_tile(slot, 4 + fc)[:, c * 128:(c + 1) * 128],
                            ident_bf),
                        reads=[qb, T("consts")], writes=[PB[6]])
                    run("tensor", lambda e, fc=fc, slot=slot, c=c, p=ps7b:
                        e.transpose(
                            p[:, fc * 128:(fc + 1) * 128],
                            qkv_tile(slot, 8 + fc)[:, c * 128:(c + 1) * 128],
                            ident_bf),
                        reads=[qb, T("consts")], writes=[PB[7]])
                run("vector", lambda e, Cg=Cg, p=ps6b, ht=ht: e.tensor_scalar(
                    ht[:, 0:512], p[:], ratio[:, Cg:Cg + 1], None, ALU.mult),
                    reads=[PB[6], T("ratio")], writes=[htb])
                run("vector", lambda e, p=ps7b, ht=ht: e.tensor_copy(
                    ht[:, 512:1024], p[:]),
                    reads=[PB[7]], writes=[htb])
                for hh in range(HN):
                    fc, off = hh // 2, 64 * (hh % 2)
                    pst = PS[0] if hh < 4 else PS[1]
                    pbk = PB[0] if hh < 4 else PB[1]
                    qcol = (hh % 4) * 128
                    run("tensor", lambda e, fc=fc, off=off, pst=pst, qcol=qcol,
                        hq=hq: e.matmul(
                            pst[:, qcol:qcol + 128],
                            hq[off:off + 64, 512 + fc * 128:512 + (fc + 1) * 128],
                            hq[off:off + 64, fc * 128:(fc + 1) * 128],
                            start=True, stop=True),
                        reads=[hqb], writes=[pbk])
                run("vector", lambda e, hA=hA: e.tensor_tensor(
                    hA[:, 0:512], PS[0][:, 0:512], masku4, ALU.mult),
                    reads=[PB[0], T("consts")], writes=[hAb])
                run("vector", lambda e, hA=hA: e.tensor_tensor(
                    hA[:, 512:1024], PS[1][:, 0:512], masku4, ALU.mult),
                    reads=[PB[1], T("consts")], writes=[hAb])

            def q_chain(c):
                par = c % 2
                hq, hqb = hebqs[par], T(f"hebq{par}")
                ht, htb = hebts[par], T(f"hebt{par}")
                hA, hAb = hebAs[par], T(f"hebA{par}")
                hC, hCb = hebCs[par], T(f"hebC{par}")
                run("vector", lambda e: e.tensor_copy(
                    bass.AP(hebWb.tensor, hebWb.offset,
                            [[hebWb.ap[0][0], 64], [128, 4], [1, 64]]),
                    hebW[0:64, :].rearrange("p (r t) -> p r t", r=4)),
                    reads=[T("hebW")], writes=[T("hebWb")])
                run("vector", lambda e: e.tensor_copy(
                    bass.AP(hebWb.tensor, hebWb.offset + 64,
                            [[hebWb.ap[0][0], 128], [128, 4], [1, 64]])[64:128],
                    hebW[64:128, :].rearrange("p (r t) -> p r t", r=4)),
                    reads=[T("hebW")], writes=[T("hebWb")])
                for fc in range(4):
                    base = fc * 128
                    run("tensor", lambda e, fc=fc, base=base, ht=ht, hA=hA:
                        e.matmul(
                            PS[2][:, base:base + 128],
                            ht[:, 512 + fc * 128:512 + (fc + 1) * 128],
                            hA[:, (2 * fc) * 128:(2 * fc + 1) * 128],
                            start=True, stop=False),
                        reads=[htb, hAb], writes=[PB[2]])
                    run("tensor", lambda e, fc=fc, base=base, hq=hq: e.matmul(
                        PS[2][:, base:base + 128],
                        hebWb[0:64, fc * 128:(fc + 1) * 128],
                        hq[0:64, fc * 128:(fc + 1) * 128],
                        start=False, stop=True),
                        reads=[T("hebWb"), hqb], writes=[PB[2]])
                    run("tensor", lambda e, fc=fc, base=base, ht=ht, hA=hA:
                        e.matmul(
                            PS[3][:, base:base + 128],
                            ht[:, 512 + fc * 128:512 + (fc + 1) * 128],
                            hA[:, (2 * fc + 1) * 128:(2 * fc + 2) * 128],
                            start=True, stop=False),
                        reads=[htb, hAb], writes=[PB[3]])
                    run("tensor", lambda e, fc=fc, base=base, hq=hq: e.matmul(
                        PS[3][:, base:base + 128],
                        hebWb[64:128, fc * 128:(fc + 1) * 128],
                        hq[64:128, fc * 128:(fc + 1) * 128],
                        start=False, stop=True),
                        reads=[T("hebWb"), hqb], writes=[PB[3]])
                run("vector", lambda e, c=c: e.tensor_copy(
                    bass.AP(hebo.tensor, hebo.offset + c * 128,
                            [[hebo.ap[0][0], 64], [512, 4], [1, 128]]),
                    PS[2][0:64, 0:512].rearrange("p (r t) -> p r t", r=4)),
                    reads=[PB[2]], writes=[T("hebo")])
                run("vector", lambda e, c=c: e.tensor_copy(
                    bass.AP(hebo.tensor, hebo.offset + c * 128,
                            [[hebo.ap[0][0], 128], [512, 4], [1, 128]])[64:128],
                    PS[3][64:128, 0:512].rearrange("p (r t) -> p r t", r=4)),
                    reads=[PB[3]], writes=[T("hebo")])
                for fc in range(4):
                    run("tensor", lambda e, fc=fc, ht=ht: e.matmul(
                        PS[4][:, fc * 64:(fc + 1) * 64][0:64],
                        ht[:, fc * 128:fc * 128 + 64],
                        ht[:, 512 + fc * 128:512 + fc * 128 + 64],
                        start=True, stop=True),
                        reads=[htb], writes=[PB[4]])
                    run("tensor", lambda e, fc=fc, ht=ht: e.matmul(
                        PS[4][64:128, fc * 64:(fc + 1) * 64],
                        ht[:, fc * 128 + 64:fc * 128 + 128],
                        ht[:, 512 + fc * 128 + 64:512 + fc * 128 + 128],
                        start=True, stop=True, tile_position=(0, 64)),
                        reads=[htb], writes=[PB[4]])
                run("vector", lambda e, hC=hC: e.scalar_tensor_tensor(
                    hebW[:], hebW[:], hC[:, 0:1], PS[4][:, 0:256],
                    ALU.mult, ALU.add),
                    reads=[hCb, PB[4]], writes=[T("hebW")])

            q_prep(0)
            for c in range(1, 4):
                q_prep(c)
                q_chain(c - 1)
            q_chain(3)
            for k in range(4):
                for lt_ in range(TPQ):
                    for dh in range(2):
                        pi = lt_ * 2 + dh
                        run("tensor", lambda e, k=k, lt_=lt_, dh=dh, pi=pi: e.matmul(
                            PS[pi][:, 0:512],
                            hebo[:, k * 512 + lt_ * 128:k * 512 + (lt_ + 1) * 128],
                            wo_res[:, k * 1024 + dh * 512:k * 1024 + (dh + 1) * 512],
                            start=(k == 0), stop=False),
                            reads=[T("hebo"), T("wores")], writes=[PB[pi]])
            for lt_ in range(TPQ):
                for dh in range(2):
                    pi = lt_ * 2 + dh
                    run("tensor", lambda e, pi=pi, dh=dh: e.matmul(
                        PS[pi][:, 0:512], ones_row_bf,
                        boutrow[:, dh * 512:(dh + 1) * 512],
                        start=False, stop=True),
                        reads=[T("boutrow"), T("orow")], writes=[PB[pi]])
            for lt_ in range(TPQ):
                tt = tq * TPQ + lt_
                for dh in range(2):
                    pi = lt_ * 2 + dh
                    run("vector", lambda e, pi=pi, dh=dh, tt=tt: e.tensor_add(
                        xa_t(tt)[:, dh * 512:(dh + 1) * 512],
                        xa_t(tt)[:, dh * 512:(dh + 1) * 512], PS[pi][:, 0:512]),
                        reads=[PB[pi]], writes=[XT[tt]])

        if debug:
            pass
            tap(192, chat, T("chat"))
            tap(208, ratio, T("ratio"))
            tap(224, rchat, T("rchat"))
        if upto.startswith("qh") or upto.startswith("heb"):
            finish_early("xa")
            return nc

        mk.barrier()
        xa_stats(m3, r3, T("m3"), T("r3"), mr3, T("mr3"))
        f1s_b = [[T(f"f1s{s}_{k}") for k in range(KD)] for s in range(2)]
        h2b = T("h2")
        wf2_b = [T(f"wf2{s}") for s in range(4)]
        yr_b = [T("yrd0"), T("yrd1"), T("yrd2"), T("yrd3")]
        for tq in range(NTQ):
            ln_apply_tq(tq, mr3, r3, xn3bf3, T("mr3"), T("r3"), xa_t, XT,
                        T("xn3bf"), gcol=g3c, bcol=b3c)
            if tq == 0:
                T("xn3bf").note_read((None, 0))
            for fg in range(8):
                sl = fg % 2
                for k in range(KD):
                    wr = [f1s_b[sl][k]] + ([ACCB] if tq == 0 and fg < 2 else [])
                    run("sync", lambda e, k=k, fg=fg, sl=sl: e.dma_start(
                        f1slab3s[sl][:, k, :],
                        wff1n[k * 128:(k + 1) * 128,
                              fg * 512:(fg + 1) * 512]),
                        reads=[T("wff1n")], writes=wr)
                for fl in range(4):
                    f = fg * 4 + fl
                    pb = PB[f % 2]
                    pst = PS[f % 2]
                    for k in range(KD):
                        run("tensor", lambda e, k=k, fl=fl, pst=pst, sl=sl:
                            e.matmul(
                                pst[:, 0:512],
                                f1slab3s[sl][:, k, fl * 128:(fl + 1) * 128],
                                xn3bf3[:, k, 0:512],
                                start=(k == 0), stop=(k == KD - 1)),
                            reads=[f1s_b[sl][k], T("xn3bf")], writes=[pb])
                    run("scalar", lambda e, f=f, pst=pst: e.activation(
                        h2_bf[:, f * 512:(f + 1) * 512], pst[:, 0:512],
                        AF.Gelu_apprx_tanh, bias=bff1c[:, f:f + 1]),
                        reads=[pb, T("ffcols")], writes=[h2b])
            for lt_ in range(TPQ):
                tt = tq * TPQ + lt_
                sl = lt_
                wr = [yr_b[sl]] + ([ACCB] if tq == 0 else [])
                run("sync", lambda e, tt=tt, sl=sl: e.dma_start(
                    yrd[:, sl * 1024:(sl + 1) * 1024],
                    ypart[tt * 128:(tt + 1) * 128, :]),
                    reads=[T("ypart")], writes=wr)
            for f in range(KF):
                sl = f % 4
                wv = wf2slab[:, sl * 1024:(sl + 1) * 1024]
                wr = [wf2_b[sl]] + ([ACCB] if tq == 0 and f < 4 else [])
                run("gpsimd", lambda e, w=wv, f=f: e.dma_start(
                    w, wff2n[f * 128:(f + 1) * 128, :]),
                    reads=[T("wff2n")], writes=wr)
                for lt_ in range(TPQ):
                    for dh in range(2):
                        pi = lt_ * 2 + dh
                        run("tensor", lambda e, f=f, lt_=lt_, dh=dh, pi=pi, w=wv:
                            e.matmul(PS[pi][:, 0:512],
                                     h2_bf[:, f * 512 + lt_ * 128:
                                           f * 512 + (lt_ + 1) * 128],
                                     w[:, dh * 512:(dh + 1) * 512],
                                     start=(f == 0), stop=False),
                            reads=[wf2_b[sl], h2b], writes=[PB[pi]])
            for lt_ in range(TPQ):
                for dh in range(2):
                    pi = lt_ * 2 + dh
                    run("tensor", lambda e, pi=pi, dh=dh: e.matmul(
                        PS[pi][:, 0:512], ones_row_bf,
                        bff2row[:, dh * 512:(dh + 1) * 512],
                        start=False, stop=True),
                        reads=[T("bff2row"), T("orow")], writes=[PB[pi]])
            for lt_ in range(TPQ):
                tt = tq * TPQ + lt_
                sl = lt_
                osl = tt % 2
                ov = out_tm[:, osl * 1024:(osl + 1) * 1024]
                ob = T(f"otm{osl}")
                for dh in range(2):
                    pi = lt_ * 2 + dh
                    run("vector", lambda e, pi=pi, dh=dh, tt=tt, o=ov: e.tensor_add(
                        o[:, dh * 512:(dh + 1) * 512], PS[pi][:, 0:512],
                        xa_t(tt)[:, dh * 512:(dh + 1) * 512]),
                        reads=[PB[pi], XT[tt]], writes=[ob])
                    run("vector", lambda e, dh=dh, sl=sl, o=ov: e.tensor_add(
                        o[:, dh * 512:(dh + 1) * 512],
                        o[:, dh * 512:(dh + 1) * 512],
                        yrd[:, sl * 1024 + dh * 512:sl * 1024 + (dh + 1) * 512]),
                        reads=[yr_b[sl]], writes=[ob])
                run("sync", lambda e, o=ov, tt=tt: e.dma_start(
                    y[tt * 128:(tt + 1) * 128, :], o),
                    reads=[ob], writes=[T("y")])

        for i in range(MK.NDMA):
            for q, e2 in (("SQ", "sync"), ("GQ", "gpsimd"), ("AQ", "scalar")):
                cntv = mk.cnt.get(f"{q}{i}", 0)
                if cntv > 0:
                    mk.raw_wait("sync", f"{q}{i}", cntv)
                    if e2 != "sync":
                        mk.raw_wait(e2, f"{q}{i}", cntv)
        mk.run_block(block)

    return nc


def kernel(**inputs):
    nc = bass.Bass("TRN2", target_bir_lowering=False)
    build(nc)
    CN = _consts()
    in_maps = []
    for core in range(B):
        m = {}
        for k, v in inputs.items():
            if k == "L0_attn_x":
                m[k] = np.ascontiguousarray(np.asarray(v)[core])
            else:
                m[k] = np.ascontiguousarray(np.asarray(v))
        for k, v in CN.items():
            m[k] = v
        in_maps.append(m)
    res = run_bass_kernel_spmd(nc, in_maps, core_ids=list(range(B)))
    out = np.stack([res.results[i]["y"] for i in range(B)], axis=0)
    return out


# version 21
# speedup vs baseline: 1.0318x; 1.0318x over previous
import numpy as np
import ml_dtypes
import concourse.bass as bass
import concourse.mybir as mybir
from concourse.bass_utils import run_bass_kernel_spmd

F32 = mybir.dt.float32
BF16 = mybir.dt.bfloat16
U8 = mybir.dt.uint8
AF = mybir.ActivationFunctionType
ALU = mybir.AluOpType
AXX = mybir.AxisListType.X

B, N, D = 8, 2048, 1024
HN, HD, FF = 8, 64, 4096
NT, KD, KF = 16, 8, 32
TQ, NTQ, TPQ = 512, 4, 4
ETA, DT_, GFLOOR, EPS = 0.1, 0.01, 0.1, 1e-5
KB = 1024
NW = float(FF * D)

SMALLS = {}

ENG2SEM = {"sync": "SQ", "gpsimd": "GQ", "tensor": "TE", "vector": "VE",
           "scalar": "SE"}
COMPUTE_SEMS = {"tensor": "TE", "vector": "VE", "scalar": "SE"}


def _consts():
    c = {}
    c["c_ident"] = np.eye(128, dtype=np.float32)
    c["c_ident_bf"] = np.eye(128, dtype=ml_dtypes.bfloat16)
    ipad = np.zeros((128, 256), dtype=np.float32)
    for k in range(128):
        ipad[k, 64 + k] = 1.0
    c["c_ipad"] = ipad
    c["c_cums"] = np.triu(np.ones((128, 128), dtype=np.float32))
    c["c_masku"] = (ETA * np.triu(np.ones((128, 128)))).astype(ml_dtypes.bfloat16)
    c["c_masku4"] = np.tile((ETA * np.triu(np.ones((128, 128)))), (1, 4)).astype(ml_dtypes.bfloat16)
    idm = np.zeros((128, 16), dtype=np.float32)
    idm[:, 0] = 1.0
    idm[:, 6] = 1.0
    idm[:, 8] = 1.0
    idm[:, 14] = 1.0
    c["c_idmat"] = idm
    edges = np.zeros((1, 16 * 128), dtype=np.float32)
    for r in range(8):
        s = 1 << r
        edges[0, r * 128:r * 128 + s] = 1.0
        edges[0, (8 + r) * 128 + 128 - s:(8 + r) * 128 + 128] = 1.0
    c["c_edges"] = edges
    c["c_ones_row"] = np.ones((1, 128), dtype=np.float32)
    c["c_ones_row_bf"] = np.ones((1, 128), dtype=ml_dtypes.bfloat16)
    c["c_ones_2048_bf"] = np.ones((1, 2048), dtype=ml_dtypes.bfloat16)
    return c


class MK:
    ENGS = ("sync", "gpsimd", "tensor", "vector", "scalar")
    NDMA = 8

    def __init__(self, nc):
        self.nc = nc
        self.ops = {e: [] for e in self.ENGS}
        self.nops = {e: 0 for e in self.ENGS}
        self.sems = {}
        self.cnt = {}
        self.pending = {e: {} for e in self.ENGS}
        self.drained = {e: -1 for e in self.ENGS}
        self.dma_idx = {"SQ": 0, "GQ": 0, "AQ": 0}

    def sem(self, name, handle):
        self.sems[name] = handle
        self.cnt[name] = 0

    def _flush(self, eng):
        pw = self.pending[eng]
        for key in sorted(pw, key=str):
            self.ops[eng].append({"wait": (key, pw[key])})
        pw.clear()

    def wait(self, eng, key, val):
        if val is None or val < 0 or (isinstance(key, str) and key in
                                      ("SQ", "GQ", "AQ")):
            return
        if key == eng:
            if eng != "tensor" and self.drained[eng] < val:
                self.drain(eng)
            return
        if key in self.ENGS:
            self.ops[key][val]["inc"] = True
        pw = self.pending[eng]
        if pw.get(key, -1) < val:
            pw[key] = val

    def raw_wait(self, eng, semname, val):
        self._flush(eng)
        self.ops[eng].append({"wait": (semname, val)})

    DMAQ = {"sync": "SQ", "gpsimd": "GQ", "scalar": "AQ"}

    def op(self, eng, fn, inc=None, dma=False):
        semname = ENG2SEM[eng]
        if dma and eng == "scalar":
            semname = "AQ"
        if semname in ("SQ", "GQ", "AQ"):
            idx = self.dma_idx[semname]
            self.dma_idx[semname] = (idx + 1) % self.NDMA
            sname = f"{semname}{idx}"
            prev = self.cnt[sname]
            if prev > 0:
                pw = self.pending[eng]
                if pw.get(sname, -1) < prev:
                    pw[sname] = prev
            self._flush(eng)
            self.cnt[sname] = prev + 16
            h = self.sems[sname]
            self.ops[eng].append({"fn": (lambda e, fn=fn, h=h:
                                         fn(e).then_inc(h, 16)), "inc": None})
            return (sname, prev + 16)
        self._flush(eng)
        rec = {"fn": fn, "inc": False}
        self.ops[eng].append(rec)
        i = self.nops[eng]
        self.nops[eng] += 1
        rec["opidx"] = len(self.ops[eng]) - 1
        return (eng, len(self.ops[eng]) - 1)

    def drain(self, eng):
        self._flush(eng)
        self.ops[eng].append({"drain": True})
        self.drained[eng] = len(self.ops[eng]) - 1

    def barrier(self):
        last = {}
        for eng in ("tensor", "vector", "scalar"):
            for i in range(len(self.ops[eng]) - 1, -1, -1):
                rec = self.ops[eng][i]
                if "fn" in rec and rec.get("inc") is not None:
                    last[eng] = i
                    break
        for eng in self.ENGS:
            for src_e, idx in last.items():
                if src_e == eng:
                    if self.drained[eng] < idx:
                        self.drain(eng)
                    continue
                self.wait(eng, src_e, idx)
            for s, v in self.cnt.items():
                if v > 0:
                    pw = self.pending[eng]
                    if pw.get(s, -1) < v:
                        pw[s] = v
            self._flush(eng)

    def run_block(self, block):
        for eng in self.ENGS:
            self._flush(eng)
        prefix = {}
        for eng in self.ENGS:
            pc, acc = [], 0
            for rec in self.ops[eng]:
                if rec.get("inc"):
                    acc += 1
                pc.append(acc)
            prefix[eng] = pc
        for eng in self.ENGS:
            ops = self.ops[eng]
            sems = self.sems
            pfx = prefix

            def body(e, ops=ops, sems=sems, pfx=pfx, eng=eng):
                for rec in ops:
                    if "drain" in rec:
                        e.drain()
                    elif "wait" in rec:
                        key, val = rec["wait"]
                        if key in self.ENGS:
                            v = pfx[key][val]
                            if v > 0:
                                e.wait_ge(sems[ENG2SEM[key]], v)
                        else:
                            e.wait_ge(sems[key], val)
                    elif rec.get("inc"):
                        h = sems[ENG2SEM[eng]]
                        rec["fn"](e).then_inc(h, 1)
                    else:
                        rec["fn"](e)

            getattr(block, eng)(body)


class Buf:
    def __init__(self, mk, name):
        self.mk = mk
        self.name = name
        self.w = None
        self.r = {}

    def note_read(self, fence):
        s, v = fence
        if self.r.get(s, 0) < v:
            self.r[s] = v


def build(nc, debug=False, upto="full"):
    mk = MK(nc)

    def din(name, shape):
        return nc.dram_tensor(name, shape, F32, kind="ExternalInput").ap()

    I = {}
    for k, nm, sh in [
        ("x0", "L0_attn_x", [N, D]),
        ("ln1_g", "L0_attn_ln1_g", [D]), ("ln1_b", "L0_attn_ln1_b", [D]),
        ("ln2_g", "L0_attn_ln2_g", [D]), ("ln2_b", "L0_attn_ln2_b", [D]),
        ("ln3_g", "L0_attn_ln3_g", [D]), ("ln3_b", "L0_attn_ln3_b", [D]),
        ("w_pot", "L0_attn_w_pot", [D, 2]), ("b_pot", "L0_attn_b_pot", [2]),
        ("w_bk", "L0_attn_w_bk", [2, D]), ("b_bk", "L0_attn_b_bk", [D]),
        ("w_qkv", "L0_attn_w_qkv", [D, 1536]), ("b_qkv", "L0_attn_b_qkv", [1536]),
        ("w_out", "L0_attn_w_out", [512, D]), ("b_out", "L0_attn_b_out", [D]),
        ("w_ff1", "L0_attn_w_ff1", [D, FF]), ("b_ff1", "L0_attn_b_ff1", [FF]),
        ("w_ff2", "L0_attn_w_ff2", [FF, D]), ("b_ff2", "L0_attn_b_ff2", [D]),
        ("fW1", "L0_ffn_W1", [FF, D]), ("fb1", "L0_ffn_b1", [FF]),
        ("fW2", "L0_ffn_W2", [D, FF]), ("fb2", "L0_ffn_b2", [D]),
    ]:
        I[k] = din(nm, sh)

    CN = _consts()
    C = {}
    for k, v in CN.items():
        dt = BF16 if v.dtype == ml_dtypes.bfloat16 else F32
        C[k] = nc.dram_tensor(k, list(v.shape), dt, kind="ExternalInput").ap()

    y = nc.dram_tensor("y", [N, D], F32, kind="ExternalOutput").ap()
    w1n = nc.dram_tensor("w1n", [FF, D], BF16, kind="Internal").ap()
    w2n = nc.dram_tensor("w2n", [D, FF], BF16, kind="Internal").ap()
    wff1n = nc.dram_tensor("wff1n", [D, FF], BF16, kind="Internal").ap()
    wff2n = nc.dram_tensor("wff2n", [FF, D], BF16, kind="Internal").ap()
    ypart = nc.dram_tensor("ypart", [N, D], F32, kind="Internal").ap()
    rsc = nc.dram_tensor("rowscratch", [8, N], F32, kind="Internal").ap()
    dbg = nc.dram_tensor("dbg", [128, 512], F32, kind="Internal").ap() if debug else None
    dbgxa = nc.dram_tensor("dbgxa", [128, 16384], F32, kind="Internal").ap() if debug else None
    dbgq = nc.dram_tensor("dbgq", [128, 6144], F32, kind="Internal").ap() if debug else None
    dbgh = nc.dram_tensor("dbgh", [128, 2048], F32, kind="Internal").ap() if debug else None

    import contextlib
    with contextlib.ExitStack() as _stack:
        _ec = _stack.enter_context
        arena = _ec(nc.sbuf_tensor("arena", [128, 185 * KB], U8))
        sbc = _ec(nc.sbuf_tensor("sbc", [128, 3712], U8))
        sbs = _ec(nc.sbuf_tensor("sbs", [128, 8192], U8))
        sbr = _ec(nc.sbuf_tensor("sbr", [128, 8192], U8))
        ps0 = _ec(nc.psum_tensor("ps0", [128, 512], F32))
        ps1 = _ec(nc.psum_tensor("ps1", [128, 512], F32))
        ps2 = _ec(nc.psum_tensor("ps2", [128, 512], F32))
        ps3 = _ec(nc.psum_tensor("ps3", [128, 512], F32))
        ps4 = _ec(nc.psum_tensor("ps4", [128, 512], F32))
        ps5 = _ec(nc.psum_tensor("ps5", [128, 512], F32))
        ps6 = _ec(nc.psum_tensor("ps6", [128, 512], F32))
        ps7 = _ec(nc.psum_tensor("ps7", [128, 512], F32))
        sem_te = _ec(nc.semaphore())
        sem_ve = _ec(nc.semaphore())
        sem_se = _ec(nc.semaphore())
        dma_sems = {}
        for qn in ("SQ", "GQ", "AQ"):
            for i in range(MK.NDMA):
                dma_sems[f"{qn}{i}"] = _ec(nc.semaphore(name=f"{qn}{i}"))
        block = _ec(nc.Block())
        PS = [ps0, ps1, ps2, ps3, ps4, ps5, ps6, ps7]
        for nm, h in [("TE", sem_te), ("VE", sem_ve), ("SE", sem_se)]:
            mk.sem(nm, h)
        for nm, h in dma_sems.items():
            mk.sem(nm, h)

        bufs = {}

        def T(name):
            if name not in bufs:
                bufs[name] = Buf(mk, name)
            return bufs[name]

        PB = [T(f"ps{i}") for i in range(8)]

        def run(eng, fn, reads=(), writes=(), inc=True, dma=False):
            for b in writes:
                if b.w is not None:
                    mk.wait(eng, b.w[0], b.w[1])
                for s, v in b.r.items():
                    mk.wait(eng, s, v)
            for b in reads:
                if b.w is not None:
                    mk.wait(eng, b.w[0], b.w[1])
            f = mk.op(eng, fn, inc=inc, dma=dma)
            for b in writes:
                b.w = f
                b.r = {}
            for b in reads:
                b.note_read(f)
            return f

        def v8(t, off, dt, cols, p0=0, npts=128):
            esz = 4 if dt == F32 else (2 if dt == BF16 else 1)
            return t[p0:p0 + npts, off:off + cols * esz].bitcast(dt)

        acc = v8(arena, 0, F32, 16 * KB)
        wq_res = v8(arena, 0, BF16, 12 * KB)
        wo_res = v8(arena, 24 * KB, BF16, 4 * KB)
        hebqs = [v8(arena, 32 * KB, BF16, 1024), v8(arena, 56 * KB, BF16, 1024)]
        hebts = [v8(arena, 34 * KB, BF16, 1024), v8(arena, 58 * KB, BF16, 1024)]
        hebAs = [v8(arena, 36 * KB, BF16, 1024), v8(arena, 60 * KB, BF16, 1024)]
        hebW = v8(arena, 38 * KB, F32, 256)
        hebWb = v8(arena, 39 * KB, BF16, 512)
        hebCs = [v8(arena, 40 * KB, F32, 1), v8(arena, 62 * KB, F32, 1)]
        hebo = v8(arena, 41 * KB, BF16, 2048)
        xn2bf = v8(arena, 48 * KB, BF16, 4096)
        xn3bf = v8(arena, 0, BF16, 4096)
        f1slabs = [v8(arena, 16 * KB, BF16, 4096), v8(arena, 24 * KB, BF16, 4096)]
        wf2slab = v8(arena, 32 * KB, BF16, 4096)
        yrd = v8(arena, 40 * KB, F32, 4096)

        xa = v8(arena, 64 * KB, F32, 16 * KB)
        xnbf = v8(arena, 64 * KB, BF16, 16 * KB)
        wstage = v8(arena, 96 * KB, F32, 2048)
        cs1 = v8(arena, 96 * KB, BF16, 4096)
        tstg = [v8(arena, 96 * KB, F32, 2048), v8(arena, 128 * KB, F32, 2048),
                v8(arena, 136 * KB, F32, 2048), v8(arena, 144 * KB, F32, 2048)]
        tbsl = [v8(arena, 104 * KB, BF16, 2048), v8(arena, 152 * KB, BF16, 2048)]
        ttn = [v8(arena, 108 * KB, BF16, 2048), v8(arena, 156 * KB, BF16, 2048)]
        wtA = v8(arena, 104 * KB, BF16, 1024)
        wtB = v8(arena, 106 * KB, BF16, 1024)
        wtern = v8(arena, 108 * KB, BF16, 2048)
        fgslab = v8(arena, 112 * KB, BF16, 8192)

        h_bf = v8(arena, 128 * KB, BF16, 16 * KB)
        h2_bf = h_bf
        qkvbf = v8(arena, 128 * KB, BF16, 12 * KB)
        vrows_tmp = v8(arena, 128 * KB, F32, 4096)
        w2slab = v8(arena, 160 * KB, BF16, 2048)
        junk = v8(arena, 164 * KB, F32, 1024)
        stg = v8(arena, 168 * KB, F32, 2048)
        out_tm = v8(arena, 176 * KB, F32, 2048)

        co = [0]

        def cvv(dt, cols):
            esz = 4 if dt == F32 else 2
            off = co[0]
            co[0] += cols * esz
            assert co[0] <= 3712
            return sbc[:, off:off + cols * esz].bitcast(dt)

        ident = cvv(F32, 128)
        ident_bf = cvv(BF16, 128)
        ipad = cvv(F32, 256)
        cums_c = cvv(F32, 128)
        masku = cvv(BF16, 128)
        masku4 = cvv(BF16, 512)
        idmat = cvv(F32, 16)
        pass

        so = [0]

        def sv(dt, cols, npts=128, p0=0, nm=None):
            esz = 4 if dt == F32 else 2
            off = so[0]
            so[0] += cols * esz
            assert so[0] <= 8192, so[0]
            if nm is not None:
                SMALLS[nm] = (off, cols, "f32" if dt == F32 else "bf16", p0, npts)
            return sbs[p0:p0 + npts, off:off + cols * esz].bitcast(dt)

        m0 = sv(F32, 16, nm="m0"); r0 = sv(F32, 16, nm="r0"); v0 = sv(F32, 16, nm="v0")
        m2 = sv(F32, 16, nm="m2"); r2 = sv(F32, 16, nm="r2")
        m3 = sv(F32, 16, nm="m3"); r3 = sv(F32, 16, nm="r3")
        mr0 = sv(F32, 16); mr2 = sv(F32, 16); mr3 = sv(F32, 16)
        r1 = sv(F32, 16, nm="r1")
        ssum = sv(F32, 16, nm="ssum"); ssq = sv(F32, 16, nm="ssq")
        st_a = sv(F32, 16); st_b = sv(F32, 16)
        epscol = sv(F32, 1); onecol = sv(F32, 1); onescolf = sv(F32, 1)
        tcols = sv(F32, 4, nm="tcols")
        csums = sv(F32, 64)
        g1c = sv(F32, 8); b1c = sv(F32, 8)
        g2c = sv(F32, 8); b2c = sv(F32, 8)
        g3c = sv(F32, 8); b3c = sv(F32, 8)
        bqkvc = sv(F32, 12)
        bff1c = sv(F32, 32)
        fb1c = sv(F32, 32)
        wpotc = sv(BF16, 16)
        wpotf = sv(F32, 16)
        cpotc = sv(F32, 2, nm="cpotc")
        tiny = sv(F32, 8)
        potV = sv(F32, 16, nm="potV"); potP = sv(F32, 16, nm="potP"); gam = sv(F32, 16, nm="gam")
        vbk = sv(F32, 16, nm="vbk"); gbk = sv(F32, 16, nm="gbk")
        s1 = {nm: sv(F32, 36, nm="s1" + nm) for nm in ("fre", "fim", "bre", "bim")}
        pk_a = sv(F32, 16, nm="pk_a"); pk_b = sv(F32, 16, nm="pk_b")
        vfc = sv(F32, 8, nm="vfc")
        vbc = sv(F32, 8, nm="vbc")
        ST = [sv(F32, 16) for _ in range(14)]
        grebk = sv(F32, 16, nm="grebk"); gimbk = sv(F32, 16, nm="gimbk")
        chat = sv(F32, 16, nm="chat"); rchat = sv(F32, 16, nm="rchat"); ratio = sv(F32, 16, nm="ratio")
        row16 = sv(F32, 256, npts=16)
        srow = sv(F32, 4, npts=1)

        def rvw(p0, npts, off, dt, cols):
            esz = 4 if dt == F32 else 2
            return sbr[p0:p0 + npts, off:off + cols * esz].bitcast(dt)

        G3 = rvw(0, 3, 0, BF16, 2048)
        wbk3 = rvw(0, 3, 4096, BF16, 1024)
        chr_ = rvw(0, 1, 6144, F32, 128)
        rchr = rvw(0, 1, 6656, F32, 128)
        cpr = rvw(0, 1, 7168, F32, 2)
        cendr = rvw(0, 1, 7176, F32, 16)
        potb2 = rvw(0, 2, 7240, F32, 1)
        ones_row = rvw(0, 1, 7680, F32, 128)
        b2row = rvw(32, 1, 0, BF16, 1024)
        boutrow = rvw(32, 1, 2048, BF16, 1024)
        bff2row = rvw(32, 1, 4096, BF16, 1024)
        ones_row_bf = rvw(32, 1, 6144, BF16, 128)
        edges_c = rvw(64, 1, 0, F32, 2048)
        idrow64 = rvw(64, 1, 8192 - 64, F32, 16)

        xnbf3 = xnbf.rearrange("p (k t) -> p k t", k=KD)
        xn2bf3 = xn2bf.rearrange("p (k t) -> p k t", k=KD)
        xn3bf3 = xn3bf.rearrange("p (k t) -> p k t", k=KD)
        fgslab3 = fgslab.rearrange("p (k f) -> p k f", k=KD)
        f1slab3s = [t.rearrange("p (k f) -> p k f", k=KD) for t in f1slabs]

        def acc_t(tt):
            return acc[:, tt * 1024:(tt + 1) * 1024]

        def xa_t(tt):
            return xa[:, tt * 1024:(tt + 1) * 1024]

        def h_tile(f):
            return h_bf[:, f * 512:(f + 1) * 512]

        def qkv_tile(slot, f):
            return qkvbf[:, slot * 6144 + f * 512:slot * 6144 + (f + 1) * 512]

        ACCB = T("acc")

        src_x = I["x0"].rearrange("(g p) d -> p g d", p=128)
        acc3 = acc.rearrange("p (g d) -> p g d", g=NT)
        for ap_, key in [(ident, "c_ident"), (ident_bf, "c_ident_bf"),
                         (ipad, "c_ipad"), (cums_c, "c_cums"),
                         (masku, "c_masku"), (masku4, "c_masku4"),
                         (idmat, "c_idmat")]:
            run("sync", lambda e, a=ap_, k=key: e.dma_start(
                a[0:C[k].shape[0], 0:C[k].shape[1]], C[k][:]),
                writes=[T("consts")])
        run("sync", lambda e: e.dma_start(ones_row, C["c_ones_row"][:]),
            writes=[T("orow")])
        run("sync", lambda e: e.dma_start(edges_c, C["c_edges"][:]),
            writes=[T("consts")])
        run("sync", lambda e: e.dma_start(idrow64, C["c_idmat"][0:1, :]),
            writes=[T("consts")])
        run("sync", lambda e: e.dma_start(ones_row_bf, C["c_ones_row_bf"][:]),
            writes=[T("orow")])
        run("gpsimd", lambda e: e.dma_start(b2row, I["fb2"][None, :]), writes=[T("b2row")])
        run("gpsimd", lambda e: e.dma_start(boutrow, I["b_out"][None, :]), writes=[T("boutrow")])
        run("gpsimd", lambda e: e.dma_start(bff2row, I["b_ff2"][None, :]), writes=[T("bff2row")])
        run("gpsimd", lambda e: e.dma_start(wbk3[0:2, :], I["w_bk"][:, :]), writes=[T("wbk3")])
        run("gpsimd", lambda e: e.dma_start(wbk3[2:3, :], I["b_bk"][None, :]), writes=[T("wbk3")])
        run("sync", lambda e: e.dma_start(potb2, I["b_pot"][:, None]), writes=[T("potb2")])
        for k in range(KD):
            run("sync", lambda e, k=k: e.dma_start(
                wpotf[:, 2 * k:2 * k + 2], I["w_pot"][k * 128:(k + 1) * 128, :]),
                writes=[T("wpotf")])
        for i, nm in enumerate(["ln1_g", "ln1_b", "ln2_g", "ln2_b", "ln3_g", "ln3_b"]):
            run("sync", lambda e, i=i, nm=nm: e.dma_start(
                vrows_tmp[i:i + 1, 0:1024], I[nm][None, :]), writes=[T("vrt")])
        run("sync", lambda e: e.dma_start(vrows_tmp[32:33, :], I["b_ff1"][None, :]),
            writes=[T("vrt")])
        run("sync", lambda e: e.dma_start(vrows_tmp[33:34, :], I["fb1"][None, :]),
            writes=[T("vrt")])
        run("sync", lambda e: e.dma_start(vrows_tmp[64:65, 0:1536], I["b_qkv"][None, :]),
            writes=[T("vrt")])
        for half in range(2):
            run("sync" if half == 0 else "gpsimd", lambda e, hh=half: e.dma_start(
                acc3[:, hh * 8:(hh + 1) * 8, :],
                src_x[:, hh * 8:(hh + 1) * 8, :]), writes=[ACCB])
        run("vector", lambda e: e.memset(epscol[:], EPS), writes=[T("epscol")])
        run("vector", lambda e: e.memset(onecol[:], 1.0), writes=[T("onecol")])
        run("vector", lambda e: e.memset(onescolf[:], 1.0), writes=[T("onescolf")])

        for k in range(KD):
            run("tensor", lambda e, k=k: e.transpose(
                PS[6][:, k * 6:(k + 1) * 6],
                vrows_tmp[0:6, k * 128:(k + 1) * 128], ident[0:6, 0:6]),
                reads=[T("vrt"), T("consts")], writes=[PB[6]])
        for i, col in enumerate([g1c, b1c, g2c, b2c, g3c, b3c]):
            run("vector", lambda e, c=col, i=i: e.tensor_copy(
                c[:, 0:8], PS[6][:, i:i + 43:6]),
                reads=[PB[6]], writes=[T("lncols")])
        for half in range(2):
            for kk in range(16):
                k = half * 16 + kk
                run("tensor", lambda e, k=k, kk=kk: e.transpose(
                    PS[7][:, kk * 2:kk * 2 + 2],
                    vrows_tmp[32:34, k * 128:(k + 1) * 128], ident[32:34, 32:34]),
                    reads=[T("vrt")], writes=[PB[7]])
            run("vector", lambda e, half=half: e.tensor_copy(
                bff1c[:, half * 16:(half + 1) * 16], PS[7][:, 0:31:2]),
                reads=[PB[7]], writes=[T("ffcols")])
            run("vector", lambda e, half=half: e.tensor_copy(
                fb1c[:, half * 16:(half + 1) * 16], PS[7][:, 1:32:2]),
                reads=[PB[7]], writes=[T("ffcols")])
        for k in range(12):
            run("tensor", lambda e, k=k: e.transpose(
                PS[6][:, 64 + k:65 + k],
                vrows_tmp[64:65, k * 128:(k + 1) * 128], ident[64:65, 64:65]),
                reads=[T("vrt")], writes=[PB[6]])
        run("vector", lambda e: e.tensor_copy(bqkvc[:], PS[6][:, 64:76]),
            reads=[PB[6]], writes=[T("bqkvc")])
        for k in range(KD):
            run("vector", lambda e, k=k: e.tensor_scalar(
                wpotc[:, 2 * k:2 * k + 2], wpotf[:, 2 * k:2 * k + 2],
                g1c[:, k:k + 1], None, ALU.mult),
                reads=[T("wpotf"), T("lncols")], writes=[T("wpotc")])

        for k in range(KD):
            run("tensor", lambda e, k=k: e.matmul(
                PS[5][0:2, 0:1], wpotf[:, 2 * k:2 * k + 2], b1c[:, k:k + 1],
                start=(k == 0), stop=(k == KD - 1)),
                reads=[T("wpotf"), T("lncols")], writes=[PB[5]])
        run("vector", lambda e: e.tensor_add(tiny[0:2, 0:1], PS[5][0:2, 0:1], potb2),
            reads=[PB[5], T("potb2")], writes=[T("tinyA")])
        run("tensor", lambda e: e.transpose(PS[5][0:1, 4:6], tiny[0:2, 0:1],
                                            ident[0:2, 0:2]),
            reads=[T("tinyA"), T("consts")], writes=[PB[5]])
        run("vector", lambda e: e.tensor_copy(cpr[:], PS[5][0:1, 4:6]),
            reads=[PB[5]], writes=[T("cpr")])
        run("tensor", lambda e: e.matmul(PS[5][:, 8:10], ones_row, cpr[:],
                                         start=True, stop=True),
            reads=[T("cpr"), T("orow")], writes=[PB[5]])
        run("vector", lambda e: e.tensor_copy(cpotc[:], PS[5][:, 8:10]),
            reads=[PB[5]], writes=[T("cpotc")])
        run("vector", lambda e: e.tensor_scalar(
            cpotc[:, 0:1], cpotc[:, 0:1], 2.0, None, ALU.add),
            reads=[], writes=[T("cpotc")])

        def w_tile_src(i):
            if i < 32:
                return I["fW1"][i * 128:(i + 1) * 128, :]
            j = i - 32
            return I["fW2"][(j // 4) * 128:(j // 4 + 1) * 128,
                            (j % 4) * 1024:(j % 4 + 1) * 1024]

        stg_b = [T(f"wstage{s}") for s in range(4)]
        bsl_b = [T("tbsl0"), T("tbsl1")]
        tern_b = [T("wtern0"), T("wtern1")]
        vb_ = T("vrt")
        _hand = stg_b[1:] + [bsl_b[1], tern_b[1]]
        for bx in _hand:
            if vb_.w is not None:
                bx.w = vb_.w
            bx.r = dict(vb_.r)

        def wd_src(di):
            if di < 16:
                return I["fW1"][di * 256:(di + 1) * 256, :].rearrange(
                    "(a p) d -> p a d", p=128)
            j = (di - 16) * 2
            r, q = j // 4, j % 4
            return I["fW2"][r * 128:(r + 1) * 128, q * 1024:(q + 2) * 1024]

        def wd_dst(di):
            if di < 16:
                return (w1n[di * 256:(di + 1) * 256, :].rearrange(
                    "(a p) d -> p a d", p=128), [T(f"w12n_w1_{di // 4}")])
            j = (di - 16) * 2
            r, q = j // 4, j % 4
            return (w2n[r * 128:(r + 1) * 128, q * 1024:(q + 2) * 1024],
                    [T(f"w12n_w2_{qq * 8 + ff}") for qq in (q, q + 1)
                     for ff in range(8)])

        def tern_scale(wi):
            lo = wi * 16
            run("vector", lambda e, wi=wi, lo=lo: e.reduce_sum(
                tiny[:, 2 + wi:3 + wi], csums[:, lo:lo + 16], axis=AXX),
                reads=[T("csums")], writes=[T("tinyB")])
            run("tensor", lambda e, wi=wi: e.matmul(
                PS[5][0:1, 12 + wi:13 + wi], onescolf, tiny[:, 2 + wi:3 + wi],
                start=True, stop=True),
                reads=[T("tinyB"), T("onescolf")], writes=[PB[5]])
            run("vector", lambda e, wi=wi: e.tensor_scalar(
                srow[:, wi:wi + 1], PS[5][0:1, 12 + wi:13 + wi],
                1.0 / NW, 1e-8, ALU.mult, ALU.max),
                reads=[PB[5]], writes=[T("srow")])
            run("vector", lambda e, wi=wi: e.tensor_scalar(
                srow[:, 2 + wi:3 + wi], srow[:, wi:wi + 1], 0.5, None,
                ALU.mult), reads=[], writes=[T("srow")])
            run("tensor", lambda e, wi=wi: e.matmul(
                PS[5][:, 16 + wi:17 + wi], ones_row, srow[:, 2 + wi:3 + wi],
                start=True, stop=True),
                reads=[T("srow"), T("orow")], writes=[PB[5]])
            run("vector", lambda e, wi=wi: e.tensor_copy(
                tcols[:, wi:wi + 1], PS[5][:, 16 + wi:17 + wi]),
                reads=[PB[5]], writes=[T("tcols")])
            run("vector", lambda e, wi=wi: e.tensor_scalar(
                tcols[:, 2 + wi:3 + wi], tcols[:, wi:wi + 1], -1.0, None,
                ALU.mult), reads=[], writes=[T("tcols")])

        def tern_p1(di, eng):
            sl = di % 4
            wv = tstg[sl]
            run(eng, lambda e, w=wv, di=di: e.dma_start(w[:], wd_src(di)),
                writes=[stg_b[sl]], dma=True)
            run("scalar", lambda e, w=wv, di=di: e.activation(
                w[:], w[:], AF.Abs, accum_out=csums[:, di:di + 1]),
                reads=[],
                writes=[stg_b[sl]] + ([T("csums")] if di in (15, 31) else []))

        def tern_p2(di, eng):
            sl = di % 4
            wv = tstg[sl]
            tv, tb = ttn[di % 2], tern_b[di % 2]
            bv, bb_ = tbsl[di % 2], bsl_b[di % 2]
            wi = 0 if di < 16 else 1
            run(eng, lambda e, w=wv, di=di: e.dma_start(w[:], wd_src(di)),
                writes=[stg_b[sl]], dma=True)
            run("vector", lambda e, w=wv, o=tv, wi=wi: e.tensor_scalar(
                o[:], w[:], tcols[:, wi:wi + 1], None, ALU.is_ge),
                reads=[stg_b[sl], T("tcols")], writes=[tb])
            run("vector", lambda e, w=wv, o=bv, wi=wi: e.tensor_scalar(
                o[:], w[:], tcols[:, 2 + wi:3 + wi], None, ALU.is_le),
                reads=[stg_b[sl], T("tcols")], writes=[bb_])
            run("vector", lambda e, t=tv, b2_=bv: e.tensor_sub(
                t[:], t[:], b2_[:]),
                reads=[tb, bb_], writes=[tb])
            dst, wrbufs = wd_dst(di)
            run("gpsimd", lambda e, d=dst, t=tv, di=di: e.dma_start(
                d, t[:] if di >= 16 else t.rearrange("p (a d) -> p a d", a=2)),
                reads=[tb], writes=wrbufs)


        def tern_rest():
            for di in range(16):
                tern_p2(di, "sync")
            for di in range(16, 32):
                tern_p1(di, "gpsimd")
            tern_scale(1)

        def tern_pass2_w2():
            for di in range(16, 32):
                tern_p2(di, "gpsimd")
            for bx in _hand + [stg_b[0], bsl_b[0], tern_b[0]]:
                hb_ = T("h")
                if bx.w is not None:
                    hb_.note_read(bx.w)
                for sname_, v_ in bx.r.items():
                    if hb_.r.get(sname_, 0) < v_:
                        hb_.r[sname_] = v_

        for di in range(16):
            tern_p1(di, "sync")

        for tt in range(NT):
            run("vector", lambda e, tt=tt: e.reduce_sum(
                ssum[:, tt:tt + 1], acc_t(tt), axis=AXX),
                reads=[ACCB], writes=[T("ssum")])
            run("scalar", lambda e, tt=tt: e.activation(
                junk[:], acc_t(tt), AF.Square, accum_out=ssq[:, tt:tt + 1]),
                reads=[ACCB], writes=[T("ssq"), T("junk")])
        run("vector", lambda e: e.tensor_scalar(m0[:], ssum[:], 1.0 / D, None,
                                                ALU.mult),
            reads=[T("ssum")], writes=[T("m0")])
        run("vector", lambda e: e.tensor_tensor(st_a[:], m0[:], m0[:], ALU.mult),
            reads=[T("m0")], writes=[T("sta")])
        run("vector", lambda e: e.scalar_tensor_tensor(
            v0[:], ssq[:], 1.0 / D, st_a[:], ALU.mult, ALU.subtract),
            reads=[T("ssq"), T("sta")], writes=[T("v0")])
        run("scalar", lambda e: e.activation(st_b[:], v0[:], AF.Ln,
                                             bias=epscol[:], scale=1.0),
            reads=[T("v0"), T("epscol")], writes=[T("stb")])
        run("scalar", lambda e: e.activation(r0[:], st_b[:], AF.Exp, scale=-0.5),
            reads=[T("stb")], writes=[T("r0")])
        run("vector", lambda e: e.tensor_scalar(st_a[:], v0[:], EPS, None, ALU.add),
            reads=[T("v0")], writes=[T("sta")])
        run("vector", lambda e: e.reciprocal(st_b[:], st_a[:]),
            reads=[T("sta")], writes=[T("stb")])
        run("vector", lambda e: e.tensor_tensor(st_a[:], v0[:], st_b[:], ALU.mult),
            reads=[T("v0"), T("stb")], writes=[T("sta")])
        run("scalar", lambda e: e.activation(st_b[:], st_a[:], AF.Ln,
                                             bias=epscol[:], scale=1.0),
            reads=[T("sta"), T("epscol")], writes=[T("stb")])
        run("scalar", lambda e: e.activation(r1[:], st_b[:], AF.Exp, scale=-0.5),
            reads=[T("stb")], writes=[T("r1")])
        run("vector", lambda e: e.tensor_tensor(mr0[:], m0[:], r0[:], ALU.mult),
            reads=[T("m0"), T("r0")], writes=[T("mr0")])
        run("vector", lambda e: e.tensor_scalar(mr0[:], mr0[:], -1.0, None,
                                                ALU.mult),
            reads=[], writes=[T("mr0")])

        tern_scale(0)

        def ln_apply_tq(tq, mrcol, rcol, dst3, mbuf, rbuf, src_t, srcbuf, dstbuf,
                        gcol=None, bcol=None, slot4=False):
            sb_of = ((lambda tt: srcbuf[tt]) if isinstance(srcbuf, list)
                     else (lambda tt: srcbuf))
            for lt_ in range(TPQ):
                tt = tq * TPQ + lt_
                if slot4 and tt % 4 >= 2:
                    sl = tt % 4 - 2
                    srot = out_tm[:, sl * 1024:(sl + 1) * 1024]
                    sb = T(f"otm{sl}")
                else:
                    sl = tt % 2
                    srot = stg[:, sl * 1024:(sl + 1) * 1024]
                    sb = T(f"stg{sl}")
                run("scalar", lambda e, s=srot, tt=tt, mr=mrcol, r=rcol,
                    src_t=src_t: e.activation(
                        s, src_t(tt), AF.Identity, scale=r[:, tt:tt + 1],
                        bias=mr[:, tt:tt + 1]),
                    reads=[sb_of(tt), mbuf, rbuf], writes=[sb])
                pb = PB[2] if sl == 0 else PB[3]
                pst = PS[2] if sl == 0 else PS[3]
                for k in range(KD):
                    run("tensor", lambda e, k=k, pst=pst, s=srot: e.transpose(
                        pst[:, (k % 4) * 128:(k % 4) * 128 + 128],
                        s[:, k * 128:(k + 1) * 128], ident),
                        reads=[sb, T("consts")], writes=[pb])
                    if k in (3, 7):
                        base = k - 3
                        if gcol is None:
                            run("scalar", lambda e, b=base, pst=pst, lt_=lt_,
                                tq=tq: e.activation(
                                    dst3[:, b:b + 4,
                                         tq * 512 + lt_ * 128:tq * 512 + (lt_ + 1) * 128]
                                    if dst3 is xnbf3 else
                                    dst3[:, b:b + 4, lt_ * 128:(lt_ + 1) * 128],
                                    pst[:, 0:512].rearrange("p (k t) -> p k t", k=4),
                                    AF.Identity),
                                reads=[pb], writes=[dstbuf])
                        else:
                            for kk in range(base, base + 4):
                                run("vector", lambda e, kk=kk, pst=pst, lt_=lt_:
                                    e.tensor_scalar(
                                        dst3[:, kk, lt_ * 128:(lt_ + 1) * 128],
                                        pst[:, (kk % 4) * 128:(kk % 4) * 128 + 128],
                                        gcol[:, kk:kk + 1], bcol[:, kk:kk + 1],
                                        ALU.mult, ALU.add),
                                    reads=[pb, T("lncols")], writes=[dstbuf])

        for tq in range(NTQ):
            ln_apply_tq(tq, mr0, r0, xnbf3, T("mr0"), T("r0"), acc_t, ACCB,
                        T("xnbf"), slot4=True)
            for lt_ in range(TPQ):
                tt = tq * TPQ + lt_
                for k in range(KD):
                    run("tensor", lambda e, k=k, tt=tt: e.matmul(
                        PS[4][:, 2 * tt:2 * tt + 2],
                        xnbf3[:, k, tt * 128:(tt + 1) * 128],
                        wpotc[:, 2 * k:2 * k + 2],
                        start=(k == 0), stop=(k == KD - 1)),
                        reads=[T("xnbf"), T("wpotc")], writes=[PB[4]])

        tern_rest()
        tern_pass2_w2()

        run("vector", lambda e: e.tensor_tensor(potV[:], PS[4][:, 0:32:2], r1[:],
                                                ALU.mult),
            reads=[PB[4], T("r1")], writes=[T("potV")])
        run("vector", lambda e: e.tensor_tensor(potP[:], PS[4][:, 1:32:2], r1[:],
                                                ALU.mult),
            reads=[PB[4], T("r1")], writes=[T("potP")])
        run("vector", lambda e: e.tensor_scalar(potV[:], potV[:], cpotc[:, 0:1],
                                                None, ALU.add),
            reads=[T("cpotc")], writes=[T("potV")])
        run("vector", lambda e: e.tensor_scalar(potP[:], potP[:], cpotc[:, 1:2],
                                                None, ALU.add),
            reads=[T("cpotc")], writes=[T("potP")])
        run("scalar", lambda e: e.activation(st_a[:], potP[:], AF.Abs),
            reads=[T("potP")], writes=[T("sta")])
        run("scalar", lambda e: e.activation(st_a[:], st_a[:], AF.Exp, scale=-1.0),
            reads=[], writes=[T("sta")])
        run("scalar", lambda e: e.activation(st_a[:], st_a[:], AF.Ln,
                                             bias=onecol[:], scale=1.0),
            reads=[T("onecol")], writes=[T("sta")])
        run("scalar", lambda e: e.activation(st_b[:], potP[:], AF.Relu),
            reads=[T("potP")], writes=[T("stb")])
        run("vector", lambda e: e.tensor_add(gam[:], st_a[:], st_b[:]),
            reads=[T("sta"), T("stb")], writes=[T("gam")])
        run("vector", lambda e: e.tensor_scalar(gam[:], gam[:], GFLOOR, None,
                                                ALU.add),
            reads=[], writes=[T("gam")])

        run("tensor", lambda e: e.transpose(PS[5][0:16, 0:128], potV[:], ident),
            reads=[T("potV"), T("consts")], writes=[PB[5]])
        run("tensor", lambda e: e.transpose(PS[5][0:16, 128:256], gam[:], ident),
            reads=[T("gam"), T("consts")], writes=[PB[5]])
        run("vector", lambda e: e.tensor_copy(row16[:, 0:256], PS[5][0:16, 0:256]),
            reads=[PB[5]], writes=[T("row16")])
        run("gpsimd", lambda e: e.dma_start(
            rsc[0:1, :].rearrange("o (p j) -> (o p) j", p=16), row16[:, 0:128]),
            reads=[T("row16")], writes=[T("rsc0")], dma=True)
        run("gpsimd", lambda e: e.dma_start(
            rsc[1:2, :].rearrange("o (p j) -> (o p) j", p=16), row16[:, 128:256]),
            reads=[T("row16")], writes=[T("rsc1")], dma=True)
        run("gpsimd", lambda e: e.dma_start(
            vbk[:], rsc[0:1, :].rearrange("o (p j) -> (o p) j", p=128)),
            reads=[T("rsc0")], writes=[T("vbk")], dma=True)
        run("gpsimd", lambda e: e.dma_start(
            gbk[:], rsc[1:2, :].rearrange("o (p j) -> (o p) j", p=128)),
            reads=[T("rsc1")], writes=[T("gbk")], dma=True)

        for nm in ("fre", "fim", "bre", "bim"):
            run("vector", lambda e, t=s1[nm]: e.memset(t[:, 0:4], 0.0),
                writes=[T("s1" + nm)])
        run("vector", lambda e: e.memset(s1["fre"][:, 1:2], 1.0), writes=[T("s1fre")])
        run("vector", lambda e: e.memset(s1["fre"][:, 2:3], 1.0), writes=[T("s1fre")])
        run("vector", lambda e: e.memset(s1["bre"][:, 1:2], 1.0), writes=[T("s1bre")])
        run("vector", lambda e: e.memset(s1["bre"][:, 2:3], 1.0), writes=[T("s1bre")])

        def bk_step(j):
            for tb in ("f", "b"):
                re_ = s1["fre" if tb == "f" else "bre"]
                im_ = s1["fim" if tb == "f" else "bim"]
                rb_ = T("s1" + ("fre" if tb == "f" else "bre"))
                ib_ = T("s1" + ("fim" if tb == "f" else "bim"))
                dj = j if tb == "f" else 15 - j
                prev = slice(2 * j + 2, 2 * j + 4)
                pprev = slice(2 * j, 2 * j + 2)
                cur = slice(2 * j + 4, 2 * j + 6)
                t0 = ST[0] if tb == "f" else ST[1]
                t1 = ST[2] if tb == "f" else ST[3]
                tb0 = T("bkt0" + tb)
                tb1 = T("bkt1" + tb)
                run("vector", lambda e, re_=re_, im_=im_, dj=dj, t0=t0, prev=prev,
                    pprev=pprev: e.scalar_tensor_tensor(
                        t0[:, 0:2], im_[:, prev], gbk[:, dj:dj + 1],
                        re_[:, pprev], ALU.mult, ALU.subtract),
                    reads=[rb_, ib_, T("gbk")], writes=[tb0])
                run("vector", lambda e, re_=re_, im_=im_, dj=dj, t1=t1, prev=prev,
                    pprev=pprev: e.scalar_tensor_tensor(
                        t1[:, 0:2], re_[:, prev], gbk[:, dj:dj + 1],
                        im_[:, pprev], ALU.mult, ALU.add),
                    reads=[rb_, ib_, T("gbk")], writes=[tb1])
                run("vector", lambda e, re_=re_, dj=dj, t0=t0, prev=prev, cur=cur:
                    e.scalar_tensor_tensor(
                        re_[:, cur], re_[:, prev], vbk[:, dj:dj + 1], t0[:, 0:2],
                        ALU.mult, ALU.add),
                    reads=[tb0, T("vbk")], writes=[rb_])
                run("vector", lambda e, im_=im_, dj=dj, t1=t1, prev=prev, cur=cur:
                    e.scalar_tensor_tensor(
                        im_[:, cur], im_[:, prev], vbk[:, dj:dj + 1], t1[:, 0:2],
                        ALU.mult, ALU.subtract),
                    reads=[tb1, T("vbk")], writes=[ib_])

        def bk_renorm(j):
            for tb in ("f", "b"):
                re_ = s1["fre" if tb == "f" else "bre"]
                im_ = s1["fim" if tb == "f" else "bim"]
                rb_ = T("s1" + ("fre" if tb == "f" else "bre"))
                ib_ = T("s1" + ("fim" if tb == "f" else "bim"))
                nb = T("nrm")
                lo = 2 * j + 2
                hi = 2 * j + 6
                run("vector", lambda e, re_=re_, lo=lo: e.tensor_reduce(
                    ST[4][:, 0:1], re_[:, lo:lo + 4], AXX, ALU.max,
                    apply_absolute_value=True), reads=[rb_], writes=[nb])
                run("vector", lambda e, im_=im_, lo=lo: e.tensor_reduce(
                    ST[4][:, 1:2], im_[:, lo:lo + 4], AXX, ALU.max,
                    apply_absolute_value=True), reads=[ib_], writes=[nb])
                run("vector", lambda e: e.tensor_max(ST[4][:, 2:3], ST[4][:, 0:1],
                                                     ST[4][:, 1:2]),
                    reads=[nb], writes=[nb])
                run("vector", lambda e: e.reciprocal(ST[4][:, 3:4], ST[4][:, 2:3]),
                    reads=[nb], writes=[nb])
                for t_, b_ in ((re_, rb_), (im_, ib_)):
                    run("vector", lambda e, t_=t_, hi=hi: e.tensor_scalar(
                        t_[:, 0:hi], t_[:, 0:hi], ST[4][:, 3:4], None,
                        ALU.mult), reads=[nb], writes=[b_])

        for j in range(16):
            bk_step(j)
            if j in (4, 9):
                bk_renorm(j)

        def pack_from_s1(dst, dstb):
            for (tb, base) in (("f", 0), ("b", 8)):
                re_ = s1["fre" if tb == "f" else "bre"]
                im_ = s1["fim" if tb == "f" else "bim"]
                rb_ = T("s1" + ("fre" if tb == "f" else "bre"))
                ib_ = T("s1" + ("fim" if tb == "f" else "bim"))
                run("vector", lambda e, d=dst, b=base, re_=re_: e.tensor_copy(
                    d[:, b:b + 4:2], re_[:, 34:36]), reads=[rb_], writes=[dstb])
                run("vector", lambda e, d=dst, b=base, im_=im_: e.tensor_copy(
                    d[:, b + 1:b + 4:2], im_[:, 34:36]), reads=[ib_], writes=[dstb])
                run("vector", lambda e, d=dst, b=base, re_=re_: e.tensor_copy(
                    d[:, b + 4:b + 8:2], re_[:, 32:34]), reads=[rb_], writes=[dstb])
                run("vector", lambda e, d=dst, b=base, im_=im_: e.tensor_copy(
                    d[:, b + 5:b + 8:2], im_[:, 32:34]), reads=[ib_], writes=[dstb])

        pack_from_s1(pk_a, T("pka"))


        def s2pair(t, col):
            return t[:, col:col + 9:8]

        def psum_pair(colf):
            return (PS[5][:, 32 + colf:33 + colf], PS[6][:, colf:colf + 1])

        def stage2_round(s, src, srcb, dst, dstb, last=False):
            r_ = s.bit_length() - 1
            run("tensor", lambda e, s=s, src=src: e.matmul(
                PS[5][:, 32:40], ipad[:, 64 - s:192 - s], src[:, 0:8],
                start=True, stop=False),
                reads=[srcb, T("consts")], writes=[PB[5]])
            run("tensor", lambda e, r_=r_: e.matmul(
                PS[5][:, 32:40], edges_c[:, r_ * 128:(r_ + 1) * 128],
                idrow64[:, 0:8], start=False, stop=True),
                reads=[T("consts")], writes=[PB[5]])
            run("tensor", lambda e, s=s, src=src: e.matmul(
                PS[6][:, 0:8], ipad[:, 64 + s:192 + s], src[:, 8:16],
                start=True, stop=False),
                reads=[srcb, T("consts")], writes=[PB[6]])
            run("tensor", lambda e, r_=r_: e.matmul(
                PS[6][:, 0:8], edges_c[:, (8 + r_) * 128:(9 + r_) * 128],
                idrow64[:, 8:16], start=False, stop=True),
                reads=[T("consts")], writes=[PB[6]])
            if last:
                run("vector", lambda e: e.tensor_copy(vfc[:, 0:2], PS[5][:, 32:34]),
                    reads=[PB[5]], writes=[T("vfc")])
                run("vector", lambda e: e.tensor_copy(vfc[:, 2:4], PS[5][:, 36:38]),
                    reads=[PB[5]], writes=[T("vfc")])
                run("vector", lambda e: e.tensor_copy(vbc[:, 0:2], PS[6][:, 0:2]),
                    reads=[PB[6]], writes=[T("vbc")])
                run("vector", lambda e: e.tensor_copy(vbc[:, 2:4], PS[6][:, 4:6]),
                    reads=[PB[6]], writes=[T("vbc")])
                return

            def emit_dir(goff_t, gslc, base):
                FA, FB, FC, FD = base + 0, base + 2, base + 4, base + 6
                pairs = [
                    (base + 0, (FA, 0, FB, 4)),
                    (base + 2, (FA, 2, FB, 6)),
                    (base + 4, (FC, 0, FD, 4)),
                    (base + 6, (FC, 2, FD, 6)),
                ]
                for (dc, (f1, gg1, f2, gg2)) in pairs:
                    run("vector", lambda e, f1=f1, g=gg1: e.tensor_tensor(
                        ST[5][:, 0:1], src[:, f1:f1 + 1], gslc(g), ALU.mult),
                        reads=[srcb, goff_t], writes=[T("stm")])
                    run("vector", lambda e, f1=f1, g=gg1: e.tensor_tensor(
                        ST[5][:, 1:2], src[:, f1 + 1:f1 + 2], gslc(g + 1), ALU.mult),
                        reads=[srcb, goff_t], writes=[T("stm")])
                    run("vector", lambda e, f2=f2, g=gg2: e.tensor_tensor(
                        ST[5][:, 2:3], src[:, f2:f2 + 1], gslc(g), ALU.mult),
                        reads=[srcb, goff_t], writes=[T("stm")])
                    run("vector", lambda e, f2=f2, g=gg2: e.tensor_tensor(
                        ST[5][:, 3:4], src[:, f2 + 1:f2 + 2], gslc(g + 1), ALU.mult),
                        reads=[srcb, goff_t], writes=[T("stm")])
                    run("vector", lambda e: e.tensor_sub(ST[6][:, 0:1],
                                                         ST[5][:, 0:1], ST[5][:, 1:2]),
                        reads=[T("stm")], writes=[T("stm2")])
                    run("vector", lambda e: e.tensor_sub(ST[6][:, 1:2],
                                                         ST[5][:, 2:3], ST[5][:, 3:4]),
                        reads=[T("stm")], writes=[T("stm2")])
                    run("vector", lambda e, dc=dc: e.tensor_add(
                        dst[:, dc:dc + 1], ST[6][:, 0:1], ST[6][:, 1:2]),
                        reads=[T("stm2")], writes=[dstb])
                    run("vector", lambda e, f1=f1, g=gg1: e.tensor_tensor(
                        ST[5][:, 0:1], src[:, f1:f1 + 1], gslc(g + 1), ALU.mult),
                        reads=[srcb, goff_t], writes=[T("stm")])
                    run("vector", lambda e, f1=f1, g=gg1: e.tensor_tensor(
                        ST[5][:, 1:2], src[:, f1 + 1:f1 + 2], gslc(g), ALU.mult),
                        reads=[srcb, goff_t], writes=[T("stm")])
                    run("vector", lambda e, f2=f2, g=gg2: e.tensor_tensor(
                        ST[5][:, 2:3], src[:, f2:f2 + 1], gslc(g + 1), ALU.mult),
                        reads=[srcb, goff_t], writes=[T("stm")])
                    run("vector", lambda e, f2=f2, g=gg2: e.tensor_tensor(
                        ST[5][:, 3:4], src[:, f2 + 1:f2 + 2], gslc(g), ALU.mult),
                        reads=[srcb, goff_t], writes=[T("stm")])
                    run("vector", lambda e: e.tensor_add(ST[6][:, 0:1],
                                                         ST[5][:, 0:1], ST[5][:, 1:2]),
                        reads=[T("stm")], writes=[T("stm2")])
                    run("vector", lambda e: e.tensor_add(ST[6][:, 1:2],
                                                         ST[5][:, 2:3], ST[5][:, 3:4]),
                        reads=[T("stm")], writes=[T("stm2")])
                    run("vector", lambda e, dc=dc: e.tensor_add(
                        dst[:, dc + 1:dc + 2], ST[6][:, 0:1], ST[6][:, 1:2]),
                        reads=[T("stm2")], writes=[dstb])

            emit_dir(PB[5], lambda g: PS[5][:, 32 + g:33 + g], 0)
            emit_dir(PB[6], lambda g: PS[6][:, g:g + 1], 8)
            for base in (0, 8):
                run("vector", lambda e, d=dst, b=base: e.tensor_reduce(
                    ST[4][:, 0:1], d[:, b:b + 8], AXX, ALU.max,
                    apply_absolute_value=True),
                    reads=[dstb], writes=[T("nrm")])
                run("vector", lambda e: e.reciprocal(ST[4][:, 1:2], ST[4][:, 0:1]),
                    reads=[T("nrm")], writes=[T("nrm")])
                run("vector", lambda e, d=dst, b=base: e.tensor_scalar(
                    d[:, b:b + 8], d[:, b:b + 8], ST[4][:, 1:2], None, ALU.mult),
                    reads=[T("nrm")], writes=[dstb])

        _bk2 = {"cur": pk_a, "curb": T("pka"), "oth": pk_b, "othb": T("pkb"),
                "done": 0}

        def emit_bk_round():
            i = _bk2["done"]
            if i > 7:
                return
            if i < 7:
                stage2_round(1 << i, _bk2["cur"], _bk2["curb"], _bk2["oth"],
                             _bk2["othb"])
                _bk2["cur"], _bk2["curb"], _bk2["oth"], _bk2["othb"] = (
                    _bk2["oth"], _bk2["othb"], _bk2["cur"], _bk2["curb"])
            else:
                stage2_round(1, _bk2["cur"], _bk2["curb"], _bk2["oth"],
                             _bk2["othb"], last=True)
            _bk2["done"] = i + 1

        fgs_b = [T(f"fgs{k}") for k in range(KD)]
        hb = T("h")
        w2s_b = [T("w2s0"), T("w2s1")]
        otm_b = [T("otm0"), T("otm1")]
        for tq in range(NTQ):
            for fg in range(4):
                for k in range(KD):
                    run("sync", lambda e, k=k, fg=fg: e.dma_start(
                        fgslab3[:, k, :],
                        w1n[fg * 1024:(fg + 1) * 1024, k * 128:(k + 1) * 128],
                        transpose=True),
                        reads=[T(f"w12n_w1_{fg}")], writes=[fgs_b[k]], dma=True)
                for fl in range(8):
                    f = fg * 8 + fl
                    pb = PB[f % 2]
                    pst = PS[f % 2]
                    for k in range(KD):
                        run("tensor", lambda e, k=k, fl=fl, tq=tq, pst=pst: e.matmul(
                            pst[:, 0:512],
                            fgslab3[:, k, fl * 128:(fl + 1) * 128],
                            xnbf3[:, k, tq * 512:(tq + 1) * 512],
                            start=(k == 0), stop=(k == KD - 1)),
                            reads=[fgs_b[k], T("xnbf")], writes=[pb])
                    run("scalar", lambda e, f=f, pst=pst: e.activation(
                        h_tile(f), pst[:, 0:512], AF.Gelu, bias=fb1c[:, f:f + 1]),
                        reads=[pb, T("ffcols")], writes=[hb])
                emit_bk_round()
            for f in range(KF):
                sl = f % 2
                wv = w2slab[:, sl * 1024:(sl + 1) * 1024]
                run("sync", lambda e, w=wv, f=f: e.dma_start(
                    w, w2n[0:1024, f * 128:(f + 1) * 128], transpose=True),
                    reads=[T(f"w12n_w2_{f}")], writes=[w2s_b[sl]], dma=True)
                for lt_ in range(TPQ):
                    for dh in range(2):
                        pi = lt_ * 2 + dh
                        run("tensor", lambda e, f=f, lt_=lt_, dh=dh, pi=pi, w=wv:
                            e.matmul(PS[pi][:, 0:512],
                                     h_tile(f)[:, lt_ * 128:(lt_ + 1) * 128],
                                     w[:, dh * 512:(dh + 1) * 512],
                                     start=(f == 0), stop=False),
                            reads=[w2s_b[sl], hb], writes=[PB[pi]])
            for lt_ in range(TPQ):
                for dh in range(2):
                    pi = lt_ * 2 + dh
                    run("tensor", lambda e, pi=pi, dh=dh: e.matmul(
                        PS[pi][:, 0:512], ones_row_bf,
                        b2row[:, dh * 512:(dh + 1) * 512],
                        start=False, stop=True),
                        reads=[T("b2row"), T("orow")], writes=[PB[pi]])
            for lt_ in range(TPQ):
                tt = tq * TPQ + lt_
                osl = tt % 2
                ov = out_tm[:, osl * 1024:(osl + 1) * 1024]
                for dh in range(2):
                    pi = lt_ * 2 + dh
                    run("vector", lambda e, pi=pi, dh=dh, o=ov, tt=tt: e.tensor_add(
                        o[:, dh * 512:(dh + 1) * 512], PS[pi][:, 0:512],
                        acc_t(tt)[:, dh * 512:(dh + 1) * 512]),
                        reads=[PB[pi], ACCB], writes=[otm_b[osl]])
                run("scalar", lambda e, o=ov, tt=tt: e.dma_start(
                    ypart[tt * 128:(tt + 1) * 128, :], o),
                    reads=[otm_b[osl]], writes=[T("ypart")], dma=True)
            for c1 in (tq * 2, tq * 2 + 1):
                run("gpsimd", lambda e, c1=c1: e.dma_start(
                    cs1[:], I["w_ff1"][c1 * 128:(c1 + 1) * 128, :]),
                    writes=[T("cs1"), stg_b[0], stg_b[1]], dma=True)
                run("gpsimd", lambda e, c1=c1: e.dma_start(
                    wff1n[c1 * 128:(c1 + 1) * 128, :], cs1[:]),
                    reads=[T("cs1")], writes=[T("wff1n")], dma=True)
            for c2 in range(tq * 8, tq * 8 + 8):
                wsb = T("wtA") if c2 % 2 == 0 else T("wtB")
                wsv = wtA if c2 % 2 == 0 else wtB
                run("gpsimd", lambda e, c2=c2, w=wsv: e.dma_start(
                    w[:], I["w_ff2"][c2 * 128:(c2 + 1) * 128, :]),
                    writes=[wsb], dma=True)
                run("gpsimd", lambda e, c2=c2, w=wsv: e.dma_start(
                    wff2n[c2 * 128:(c2 + 1) * 128, :], w[:]),
                    reads=[wsb], writes=[T("wff2n")], dma=True)

        while _bk2["done"] <= 7:
            emit_bk_round()


        run("vector", lambda e: e.tensor_scalar(vfc[:, 4:8], vfc[:, 0:4], -1.0,
                                                None, ALU.mult),
            reads=[], writes=[T("vfc")])
        run("vector", lambda e: e.tensor_scalar(vbc[:, 4:8], vbc[:, 0:4], -1.0,
                                                None, ALU.mult),
            reads=[], writes=[T("vbc")])

        def stage3_dir(tb, vv, vvb, numre, numim, denre, denim, ob):
            re_ = s1["fre" if tb == "f" else "bre"]
            im_ = s1["fim" if tb == "f" else "bim"]
            rb_ = T("s1" + ("fre" if tb == "f" else "bre"))
            ib_ = T("s1" + ("fim" if tb == "f" else "bim"))
            if tb == "f":
                Are, Bre = re_[:, 4:36:2], re_[:, 5:36:2]
                Aim, Bim = im_[:, 4:36:2], im_[:, 5:36:2]
                Cre, Dre = re_[:, 2:34:2], re_[:, 3:34:2]
                Cim, Dim = im_[:, 2:34:2], im_[:, 3:34:2]
            else:
                Are, Bre = re_[:, 34:2:-2], re_[:, 35:3:-2]
                Aim, Bim = im_[:, 34:2:-2], im_[:, 35:3:-2]
                Cre, Dre = re_[:, 32:0:-2], re_[:, 33:1:-2]
                Cim, Dim = im_[:, 32:0:-2], im_[:, 33:1:-2]
            specs = [
                (numre, (Are, Aim, Bre, Bim), (0, 5, 2, 7)),
                (numim, (Are, Aim, Bre, Bim), (1, 0, 3, 2)),
                (denre, (Cre, Cim, Dre, Dim), (0, 5, 2, 7)),
                (denim, (Cre, Cim, Dre, Dim), (1, 0, 3, 2)),
            ]
            for (out_, P, Q) in specs:
                run("vector", lambda e, o=out_, P=P, Q=Q, vv=vv: e.tensor_scalar(
                    o[:], P[0], vv[:, Q[0]:Q[0] + 1], None, ALU.mult),
                    reads=[rb_, ib_, vvb], writes=[ob])
                for idx in (1, 2, 3):
                    run("vector", lambda e, o=out_, P=P, Q=Q, idx=idx, vv=vv:
                        e.scalar_tensor_tensor(
                            o[:], P[idx], vv[:, Q[idx]:Q[idx] + 1], o[:],
                            ALU.mult, ALU.add),
                        reads=[rb_, ib_, vvb], writes=[ob])

        nf_re, nf_im, df_re, df_im = ST[0], ST[1], ST[2], ST[3]
        nb_re, nb_im, db_re, db_im = ST[5], ST[6], ST[7], ST[8]
        stage3_dir("f", vfc, T("vfc"), nf_re, nf_im, df_re, df_im, T("s3f"))
        stage3_dir("b", vbc, T("vbc"), nb_re, nb_im, db_re, db_im, T("s3b"))
        s3f, s3b = T("s3f"), T("s3b")
        wd_re, wd_im = ST[9], ST[10]
        t1_re, t1_im = ST[11], ST[12]
        TMP = ST[13]

        def vv_mult(o, a, b_, rd, wr):
            run("vector", lambda e, o=o, a=a, b_=b_: e.tensor_tensor(
                o[:], a[:], b_[:], ALU.mult), reads=rd, writes=[wr])

        def vv_addsub(o, a, b_, op, rd, wr):
            run("vector", lambda e, o=o, a=a, b_=b_, op=op: e.tensor_tensor(
                o[:], a[:], b_[:], op), reads=rd, writes=[wr])

        vv_mult(wd_re, df_re, db_re, [s3f, s3b], T("wd"))
        vv_mult(TMP, df_im, db_im, [s3f, s3b], T("cmt"))
        vv_addsub(wd_re, wd_re, TMP, ALU.subtract, [T("cmt"), T("wd")], T("wd"))
        vv_mult(wd_im, df_re, db_im, [s3f, s3b], T("wd"))
        vv_mult(TMP, df_im, db_re, [s3f, s3b], T("cmt"))
        vv_addsub(wd_im, wd_im, TMP, ALU.add, [T("cmt"), T("wd")], T("wd"))
        vv_mult(t1_re, nf_re, db_re, [s3f, s3b], T("t1"))
        vv_mult(TMP, nf_im, db_im, [s3f, s3b], T("cmt"))
        vv_addsub(t1_re, t1_re, TMP, ALU.subtract, [T("cmt"), T("t1")], T("t1"))
        vv_mult(t1_im, nf_re, db_im, [s3f, s3b], T("t1"))
        vv_mult(TMP, nf_im, db_re, [s3f, s3b], T("cmt"))
        vv_addsub(t1_im, t1_im, TMP, ALU.add, [T("cmt"), T("t1")], T("t1"))
        for (a, b_, dstc, op) in ((nb_re, df_re, t1_re, ALU.add),
                                  (nb_im, df_im, t1_re, ALU.subtract),
                                  (nb_re, df_im, t1_im, ALU.add),
                                  (nb_im, df_re, t1_im, ALU.add)):
            vv_mult(TMP, a, b_, [s3f, s3b], T("cmt"))
            vv_addsub(dstc, dstc, TMP, op, [T("cmt"), T("t1")], T("t1"))
        vv_mult(nf_re, vbk, wd_re, [T("vbk"), T("wd")], T("dw"))
        vv_mult(TMP, gbk, wd_im, [T("gbk"), T("wd")], T("cmt"))
        vv_addsub(nf_re, nf_re, TMP, ALU.add, [T("cmt"), T("dw")], T("dw"))
        vv_mult(nf_im, vbk, wd_im, [T("vbk"), T("wd")], T("dw"))
        vv_mult(TMP, gbk, wd_re, [T("gbk"), T("wd")], T("cmt"))
        vv_addsub(nf_im, nf_im, TMP, ALU.subtract, [T("cmt"), T("dw")], T("dw"))
        vv_addsub(t1_re, t1_re, nf_re, ALU.subtract, [T("dw"), T("t1")], T("t1"))
        vv_addsub(t1_im, t1_im, nf_im, ALU.subtract, [T("dw"), T("t1")], T("t1"))
        vv_mult(df_re, t1_re, t1_re, [T("t1")], T("mag"))
        vv_mult(TMP, t1_im, t1_im, [T("t1")], T("cmt"))
        vv_addsub(df_re, df_re, TMP, ALU.add, [T("cmt"), T("mag")], T("mag"))
        run("vector", lambda e: e.reciprocal(df_im[:], df_re[:]),
            reads=[T("mag")], writes=[T("rmag")])
        vv_mult(grebk, wd_re, t1_re, [T("wd"), T("t1")], T("grebk"))
        vv_mult(TMP, wd_im, t1_im, [T("wd"), T("t1")], T("cmt"))
        vv_addsub(grebk, grebk, TMP, ALU.add, [T("cmt"), T("grebk")], T("grebk"))
        vv_mult(grebk, grebk, df_im, [T("rmag"), T("grebk")], T("grebk"))
        vv_mult(gimbk, wd_im, t1_re, [T("wd"), T("t1")], T("gimbk"))
        vv_mult(TMP, wd_re, t1_im, [T("wd"), T("t1")], T("cmt"))
        vv_addsub(gimbk, gimbk, TMP, ALU.subtract, [T("cmt"), T("gimbk")], T("gimbk"))
        vv_mult(gimbk, gimbk, df_im, [T("rmag"), T("gimbk")], T("gimbk"))

        run("scalar", lambda e: e.dma_start(
            rsc[2:3, :].rearrange("o (p j) -> (o p) j", p=128), grebk[:]),
            reads=[T("grebk")], writes=[T("rsc2")], dma=True)
        run("scalar", lambda e: e.dma_start(
            rsc[3:4, :].rearrange("o (p j) -> (o p) j", p=128), gimbk[:]),
            reads=[T("gimbk")], writes=[T("rsc3")], dma=True)
        run("gpsimd", lambda e: e.dma_start(G3[0:1, :], rsc[2:3, :]),
            reads=[T("rsc2")], writes=[T("G3")])
        run("gpsimd", lambda e: e.dma_start(G3[1:2, :], rsc[3:4, :]),
            reads=[T("rsc3")], writes=[T("G3")])
        run("sync", lambda e: e.dma_start(G3[2:3, :], C["c_ones_2048_bf"][:]),
            writes=[T("G3")])

        def tap(col, t, b, n=16):
            if debug:
                run("gpsimd", lambda e, col=col, t=t, n=n: e.dma_start(
                    dbg[:, col:col + n], t[:, 0:n]), reads=[b], writes=[T("dbg")])

        if debug:
            tap(0, potV, T("potV"))
            tap(16, potP, T("potP"))
            tap(32, gam, T("gam"))
            tap(48, vbk, T("vbk"))
            tap(64, gbk, T("gbk"))
            tap(80, grebk, T("grebk"))
            tap(96, gimbk, T("gimbk"))
            tap(112, r1, T("r1"))
            tap(128, m0, T("m0"))
            tap(144, r0, T("r0"))

        def finish_early(src_kind):
            if src_kind == "ypart":
                for tt in range(NT):
                    run("sync", lambda e, tt=tt: e.dma_start(
                        y[tt * 128:(tt + 1) * 128, :],
                        ypart[tt * 128:(tt + 1) * 128, :]),
                        reads=[T("ypart")], writes=[T("y")])
            elif src_kind == "xa":
                for tt in range(NT):
                    run("sync", lambda e, tt=tt: e.dma_start(
                        y[tt * 128:(tt + 1) * 128, :], xa_t(tt)),
                        reads=[XT[tt]], writes=[T("y")])
            for i in range(MK.NDMA):
                for q, e2 in (("SQ", "sync"), ("GQ", "gpsimd"), ("AQ", "scalar")):
                    cntv = mk.cnt.get(f"{q}{i}", 0)
                    if cntv > 0:
                        mk.raw_wait("sync", f"{q}{i}", cntv)
                        if e2 != "sync":
                            mk.raw_wait(e2, f"{q}{i}", cntv)
            mk.run_block(block)

        if upto == "ffn":
            finish_early("ypart")
            return nc

        mk.barrier()
        XT = [T(f"xa{t}") for t in range(NT)]
        for tt in range(NT):
            run("tensor", lambda e, tt=tt: e.matmul(
                PS[2][:, 0:512], G3[:, tt * 128:(tt + 1) * 128], wbk3[:, 0:512],
                start=True, stop=True),
                reads=[T("G3"), T("wbk3")], writes=[PB[2]])
            run("tensor", lambda e, tt=tt: e.matmul(
                PS[3][:, 0:512], G3[:, tt * 128:(tt + 1) * 128], wbk3[:, 512:1024],
                start=True, stop=True),
                reads=[T("G3"), T("wbk3")], writes=[PB[3]])
            run("vector", lambda e, tt=tt: e.tensor_scalar(
                xa_t(tt), acc_t(tt), m0[:, tt:tt + 1], r0[:, tt:tt + 1],
                ALU.subtract, ALU.mult),
                reads=[ACCB, T("m0"), T("r0")], writes=[XT[tt]])
            run("vector", lambda e, tt=tt: e.tensor_add(
                xa_t(tt)[:, 0:512], xa_t(tt)[:, 0:512], PS[2][:, 0:512]),
                reads=[PB[2]], writes=[XT[tt]])
            run("vector", lambda e, tt=tt: e.tensor_add(
                xa_t(tt)[:, 512:1024], xa_t(tt)[:, 512:1024], PS[3][:, 0:512]),
                reads=[PB[3]], writes=[XT[tt]])

        def xa_stats(mcol, rcol, mb, rb, mrcol, mrb):
            for tt in range(NT):
                run("vector", lambda e, tt=tt: e.reduce_sum(
                    ssum[:, tt:tt + 1], xa_t(tt), axis=AXX),
                    reads=[XT[tt]], writes=[T("ssum")])
                run("scalar", lambda e, tt=tt: e.activation(
                    junk[:], xa_t(tt), AF.Square, accum_out=ssq[:, tt:tt + 1]),
                    reads=[XT[tt]], writes=[T("ssq"), T("junk")])
            run("vector", lambda e, m=mcol: e.tensor_scalar(
                m[:], ssum[:], 1.0 / D, None, ALU.mult),
                reads=[T("ssum")], writes=[mb])
            run("vector", lambda e, m=mcol: e.tensor_tensor(
                st_a[:], m[:], m[:], ALU.mult), reads=[mb], writes=[T("sta")])
            run("vector", lambda e: e.scalar_tensor_tensor(
                st_b[:], ssq[:], 1.0 / D, st_a[:], ALU.mult, ALU.subtract),
                reads=[T("ssq"), T("sta")], writes=[T("stb")])
            run("scalar", lambda e: e.activation(st_a[:], st_b[:], AF.Ln,
                                                 bias=epscol[:], scale=1.0),
                reads=[T("stb"), T("epscol")], writes=[T("sta")])
            run("scalar", lambda e, r=rcol: e.activation(r[:], st_a[:], AF.Exp,
                                                         scale=-0.5),
                reads=[T("sta")], writes=[rb])
            run("vector", lambda e, m=mcol, r=rcol, mr=mrcol: e.tensor_tensor(
                mr[:], m[:], r[:], ALU.mult), reads=[mb, rb], writes=[mrb])
            run("vector", lambda e, mr=mrcol: e.tensor_scalar(
                mr[:], mr[:], -1.0, None, ALU.mult), reads=[], writes=[mrb])

        xa_stats(m2, r2, T("m2"), T("r2"), mr2, T("mr2"))
        if debug:
            tap(160, m2, T("m2"))
            tap(176, r2, T("r2"))
            run("gpsimd", lambda e: e.dma_start(dbgxa[:, 0:16384], xa[:]),
                reads=XT, writes=[T("dbg")])

        if upto == "xa":
            finish_early("xa")
            return nc

        mk.barrier()
        run("gpsimd", lambda e: e.dma_start(
            wq_res[:, 0:1536], I["w_qkv"][0:128, :]),
            writes=[T("wqres"), ACCB])
        for k in range(1, KD):
            run("gpsimd", lambda e, k=k: e.dma_start(
                wq_res[:, k * 1536:(k + 1) * 1536],
                I["w_qkv"][k * 128:(k + 1) * 128, :]),
                writes=[T("wqres")])
        run("gpsimd", lambda e: e.dma_start(
            wo_res[:, 0:1024], I["w_out"][0:128, :]),
            writes=[T("wores"), ACCB])
        for k in range(1, 4):
            run("gpsimd", lambda e, k=k: e.dma_start(
                wo_res[:, k * 1024:(k + 1) * 1024],
                I["w_out"][k * 128:(k + 1) * 128, :]),
                writes=[T("wores")])
        run("tensor", lambda e: e.matmul(PS[5][:, 64:80], cums_c, gam[:],
                                         start=True, stop=True),
            reads=[T("gam"), T("consts")], writes=[PB[5]])
        run("scalar", lambda e: e.activation(chat[:], PS[5][:, 64:80], AF.Exp,
                                             scale=-DT_),
            reads=[PB[5]], writes=[T("chat")])
        run("vector", lambda e: e.reciprocal(rchat[:], chat[:]),
            reads=[T("chat")], writes=[T("rchat")])
        run("tensor", lambda e: e.transpose(PS[5][0:16, 80:208], chat[:], ident),
            reads=[T("chat"), T("consts")], writes=[PB[5]])
        run("vector", lambda e: e.tensor_copy(row16[:, 0:128], PS[5][0:16, 80:208]),
            reads=[PB[5]], writes=[T("row16")])
        run("tensor", lambda e: e.transpose(PS[5][0:1, 208:224],
                                            row16[:, 127:128], ident[0:16, 0:16]),
            reads=[T("row16"), T("consts")], writes=[PB[5]])
        run("vector", lambda e: e.tensor_copy(cendr[:], PS[5][0:1, 208:224]),
            reads=[PB[5]], writes=[T("cendr")])
        run("tensor", lambda e: e.matmul(PS[5][:, 224:240], ones_row, cendr[:],
                                         start=True, stop=True),
            reads=[T("cendr"), T("orow")], writes=[PB[5]])
        run("vector", lambda e: e.tensor_scalar(ratio[:], PS[5][:, 224:240],
                                                ETA, None, ALU.mult),
            reads=[PB[5]], writes=[T("ratio")])
        run("vector", lambda e: e.tensor_tensor(ratio[:], ratio[:], rchat[:],
                                                ALU.mult),
            reads=[T("rchat")], writes=[T("ratio")])
        run("vector", lambda e: e.memset(hebW[:], 0.0), writes=[T("hebW"), ACCB])
        run("vector", lambda e: e.memset(hebWb[:], 0.0), writes=[T("hebWb"), ACCB])

        for tq in range(NTQ):
            if upto == "qh_prep" :
                break
            slot = tq % 2
            qb = T(f"qkv{slot}")
            ln_apply_tq(tq, mr2, r2, xn2bf3, T("mr2"), T("r2"), xa_t, XT,
                        T("xn2bf"), gcol=g2c, bcol=b2c)
            if tq == 0:
                T("xn2bf").note_read((None, 0))
            for f in range(12):
                pb = PB[f % 2]
                pst = PS[f % 2]
                for k in range(KD):
                    run("tensor", lambda e, f=f, k=k, pst=pst: e.matmul(
                        pst[:, 0:512],
                        wq_res[:, k * 1536 + f * 128:k * 1536 + (f + 1) * 128],
                        xn2bf3[:, k, 0:512],
                        start=(k == 0), stop=(k == KD - 1)),
                        reads=[T("wqres"), T("xn2bf")], writes=[pb])
                run("scalar", lambda e, f=f, pst=pst, slot=slot: e.activation(
                    qkv_tile(slot, f), pst[:, 0:512], AF.Identity,
                    bias=bqkvc[:, f:f + 1]),
                    reads=[pb, T("bqkvc")], writes=[qb])
            if debug and tq == 0:
                run("gpsimd", lambda e: e.dma_start(dbgq[:, 0:6144],
                    qkvbf[:, 0:6144]), reads=[T("qkv0")], writes=[T("dbg")])
            if upto == "qh_qkv":
                continue
            def q_prep(c, slot=slot, qb=qb, tq=tq):
                Cg = tq * 4 + c
                par = c % 2
                hq, hqb = hebqs[par], T(f"hebq{par}")
                ht, htb = hebts[par], T(f"hebt{par}")
                hA, hAb = hebAs[par], T(f"hebA{par}")
                hC, hCb = hebCs[par], T(f"hebC{par}")
                pA = [PB[0], PB[1]]
                run("tensor", lambda e, Cg=Cg: e.transpose(
                    PS[5][0:1, 256:384], chat[:, Cg:Cg + 1], ident),
                    reads=[T("chat"), T("consts")], writes=[PB[5]])
                run("vector", lambda e: e.tensor_copy(chr_[:], PS[5][0:1, 256:384]),
                    reads=[PB[5]], writes=[T("chr")])
                run("tensor", lambda e, Cg=Cg: e.transpose(
                    PS[5][0:1, 384:512], rchat[:, Cg:Cg + 1], ident),
                    reads=[T("rchat"), T("consts")], writes=[PB[5]])
                run("vector", lambda e: e.tensor_copy(rchr[:], PS[5][0:1, 384:512]),
                    reads=[PB[5]], writes=[T("rchr")])
                run("tensor", lambda e: e.matmul(PS[5][:, 0:128], ones_row,
                                                 chr_[:], start=True, stop=True),
                    reads=[T("chr"), T("orow")], writes=[PB[5]])
                run("tensor", lambda e: e.matmul(PS[5][:, 128:256], ones_row,
                                                 rchr[:], start=True, stop=True),
                    reads=[T("rchr"), T("orow")], writes=[PB[5]])
                run("vector", lambda e, hC=hC: e.tensor_copy(
                    hC[:], PS[5][:, 127:128]),
                    reads=[PB[5]], writes=[hCb])
                qsrc0 = qkvbf[:]
                cbrep = bass.AP(PS[5][:].tensor, PS[5][:].offset,
                                [PS[5][:].ap[0], [0, 4], [1, 128]])
                rbrep = bass.AP(PS[5][:].tensor, PS[5][:].offset + 128,
                                [PS[5][:].ap[0], [0, 4], [1, 128]])
                run("vector", lambda e, slot=slot, c=c, cbrep=cbrep, hq=hq:
                    e.tensor_tensor(
                        hq.rearrange("p (r t) -> p r t", r=8)[:, 0:4, :],
                        bass.AP(qsrc0.tensor, qsrc0.offset + slot * 6144 + c * 128,
                                [qsrc0.ap[0], [512, 4], [1, 128]]),
                        cbrep, ALU.mult),
                    reads=[qb, PB[5]], writes=[hqb])
                run("vector", lambda e, slot=slot, c=c, rbrep=rbrep, hq=hq:
                    e.tensor_tensor(
                        hq.rearrange("p (r t) -> p r t", r=8)[:, 4:8, :],
                        bass.AP(qsrc0.tensor,
                                qsrc0.offset + slot * 6144 + 2048 + c * 128,
                                [qsrc0.ap[0], [512, 4], [1, 128]]),
                        rbrep, ALU.mult),
                    reads=[qb, PB[5]], writes=[hqb])
                ps6b = PS[6][:, 0:256].bitcast(BF16)
                ps7b = PS[7][:, 0:256].bitcast(BF16)
                for fc in range(4):
                    run("tensor", lambda e, fc=fc, slot=slot, c=c, p=ps6b:
                        e.transpose(
                            p[:, fc * 128:(fc + 1) * 128],
                            qkv_tile(slot, 4 + fc)[:, c * 128:(c + 1) * 128],
                            ident_bf),
                        reads=[qb, T("consts")], writes=[PB[6]])
                    run("tensor", lambda e, fc=fc, slot=slot, c=c, p=ps7b:
                        e.transpose(
                            p[:, fc * 128:(fc + 1) * 128],
                            qkv_tile(slot, 8 + fc)[:, c * 128:(c + 1) * 128],
                            ident_bf),
                        reads=[qb, T("consts")], writes=[PB[7]])
                run("vector", lambda e, Cg=Cg, p=ps6b, ht=ht: e.tensor_scalar(
                    ht[:, 0:512], p[:], ratio[:, Cg:Cg + 1], None, ALU.mult),
                    reads=[PB[6], T("ratio")], writes=[htb])
                run("vector", lambda e, p=ps7b, ht=ht: e.tensor_copy(
                    ht[:, 512:1024], p[:]),
                    reads=[PB[7]], writes=[htb])
                for hh in range(HN):
                    fc, off = hh // 2, 64 * (hh % 2)
                    pst = PS[0] if hh < 4 else PS[1]
                    pbk = PB[0] if hh < 4 else PB[1]
                    qcol = (hh % 4) * 128
                    run("tensor", lambda e, fc=fc, off=off, pst=pst, qcol=qcol,
                        hq=hq: e.matmul(
                            pst[:, qcol:qcol + 128],
                            hq[off:off + 64, 512 + fc * 128:512 + (fc + 1) * 128],
                            hq[off:off + 64, fc * 128:(fc + 1) * 128],
                            start=True, stop=True),
                        reads=[hqb], writes=[pbk])
                run("vector", lambda e, hA=hA: e.tensor_tensor(
                    hA[:, 0:512], PS[0][:, 0:512], masku4, ALU.mult),
                    reads=[PB[0], T("consts")], writes=[hAb])
                run("vector", lambda e, hA=hA: e.tensor_tensor(
                    hA[:, 512:1024], PS[1][:, 0:512], masku4, ALU.mult),
                    reads=[PB[1], T("consts")], writes=[hAb])

            def q_chain(c):
                par = c % 2
                hq, hqb = hebqs[par], T(f"hebq{par}")
                ht, htb = hebts[par], T(f"hebt{par}")
                hA, hAb = hebAs[par], T(f"hebA{par}")
                hC, hCb = hebCs[par], T(f"hebC{par}")
                run("vector", lambda e: e.tensor_copy(
                    bass.AP(hebWb.tensor, hebWb.offset,
                            [[hebWb.ap[0][0], 64], [128, 4], [1, 64]]),
                    hebW[0:64, :].rearrange("p (r t) -> p r t", r=4)),
                    reads=[T("hebW")], writes=[T("hebWb")])
                run("vector", lambda e: e.tensor_copy(
                    bass.AP(hebWb.tensor, hebWb.offset + 64,
                            [[hebWb.ap[0][0], 128], [128, 4], [1, 64]])[64:128],
                    hebW[64:128, :].rearrange("p (r t) -> p r t", r=4)),
                    reads=[T("hebW")], writes=[T("hebWb")])
                for fc in range(4):
                    base = fc * 128
                    run("tensor", lambda e, fc=fc, base=base, ht=ht, hA=hA:
                        e.matmul(
                            PS[2][:, base:base + 128],
                            ht[:, 512 + fc * 128:512 + (fc + 1) * 128],
                            hA[:, (2 * fc) * 128:(2 * fc + 1) * 128],
                            start=True, stop=False),
                        reads=[htb, hAb], writes=[PB[2]])
                    run("tensor", lambda e, fc=fc, base=base, hq=hq: e.matmul(
                        PS[2][:, base:base + 128],
                        hebWb[0:64, fc * 128:(fc + 1) * 128],
                        hq[0:64, fc * 128:(fc + 1) * 128],
                        start=False, stop=True),
                        reads=[T("hebWb"), hqb], writes=[PB[2]])
                    run("tensor", lambda e, fc=fc, base=base, ht=ht, hA=hA:
                        e.matmul(
                            PS[3][:, base:base + 128],
                            ht[:, 512 + fc * 128:512 + (fc + 1) * 128],
                            hA[:, (2 * fc + 1) * 128:(2 * fc + 2) * 128],
                            start=True, stop=False),
                        reads=[htb, hAb], writes=[PB[3]])
                    run("tensor", lambda e, fc=fc, base=base, hq=hq: e.matmul(
                        PS[3][:, base:base + 128],
                        hebWb[64:128, fc * 128:(fc + 1) * 128],
                        hq[64:128, fc * 128:(fc + 1) * 128],
                        start=False, stop=True),
                        reads=[T("hebWb"), hqb], writes=[PB[3]])
                run("vector", lambda e, c=c: e.tensor_copy(
                    bass.AP(hebo.tensor, hebo.offset + c * 128,
                            [[hebo.ap[0][0], 64], [512, 4], [1, 128]]),
                    PS[2][0:64, 0:512].rearrange("p (r t) -> p r t", r=4)),
                    reads=[PB[2]], writes=[T("hebo")])
                run("vector", lambda e, c=c: e.tensor_copy(
                    bass.AP(hebo.tensor, hebo.offset + c * 128,
                            [[hebo.ap[0][0], 128], [512, 4], [1, 128]])[64:128],
                    PS[3][64:128, 0:512].rearrange("p (r t) -> p r t", r=4)),
                    reads=[PB[3]], writes=[T("hebo")])
                for fc in range(4):
                    run("tensor", lambda e, fc=fc, ht=ht: e.matmul(
                        PS[4][:, fc * 64:(fc + 1) * 64][0:64],
                        ht[:, fc * 128:fc * 128 + 64],
                        ht[:, 512 + fc * 128:512 + fc * 128 + 64],
                        start=True, stop=True),
                        reads=[htb], writes=[PB[4]])
                    run("tensor", lambda e, fc=fc, ht=ht: e.matmul(
                        PS[4][64:128, fc * 64:(fc + 1) * 64],
                        ht[:, fc * 128 + 64:fc * 128 + 128],
                        ht[:, 512 + fc * 128 + 64:512 + fc * 128 + 128],
                        start=True, stop=True, tile_position=(0, 64)),
                        reads=[htb], writes=[PB[4]])
                run("vector", lambda e, hC=hC: e.scalar_tensor_tensor(
                    hebW[:], hebW[:], hC[:, 0:1], PS[4][:, 0:256],
                    ALU.mult, ALU.add),
                    reads=[hCb, PB[4]], writes=[T("hebW")])

            q_prep(0)
            for c in range(1, 4):
                q_prep(c)
                q_chain(c - 1)
            q_chain(3)
            for k in range(4):
                for lt_ in range(TPQ):
                    for dh in range(2):
                        pi = lt_ * 2 + dh
                        run("tensor", lambda e, k=k, lt_=lt_, dh=dh, pi=pi: e.matmul(
                            PS[pi][:, 0:512],
                            hebo[:, k * 512 + lt_ * 128:k * 512 + (lt_ + 1) * 128],
                            wo_res[:, k * 1024 + dh * 512:k * 1024 + (dh + 1) * 512],
                            start=(k == 0), stop=False),
                            reads=[T("hebo"), T("wores")], writes=[PB[pi]])
            for lt_ in range(TPQ):
                for dh in range(2):
                    pi = lt_ * 2 + dh
                    run("tensor", lambda e, pi=pi, dh=dh: e.matmul(
                        PS[pi][:, 0:512], ones_row_bf,
                        boutrow[:, dh * 512:(dh + 1) * 512],
                        start=False, stop=True),
                        reads=[T("boutrow"), T("orow")], writes=[PB[pi]])
            for lt_ in range(TPQ):
                tt = tq * TPQ + lt_
                for dh in range(2):
                    pi = lt_ * 2 + dh
                    run("vector", lambda e, pi=pi, dh=dh, tt=tt: e.tensor_add(
                        xa_t(tt)[:, dh * 512:(dh + 1) * 512],
                        xa_t(tt)[:, dh * 512:(dh + 1) * 512], PS[pi][:, 0:512]),
                        reads=[PB[pi]], writes=[XT[tt]])

        if debug:
            pass
            tap(192, chat, T("chat"))
            tap(208, ratio, T("ratio"))
            tap(224, rchat, T("rchat"))
        if upto.startswith("qh") or upto.startswith("heb"):
            finish_early("xa")
            return nc

        mk.barrier()
        xa_stats(m3, r3, T("m3"), T("r3"), mr3, T("mr3"))
        f1s_b = [[T(f"f1s{s}_{k}") for k in range(KD)] for s in range(2)]
        h2b = T("h2")
        wf2_b = [T(f"wf2{s}") for s in range(4)]
        yr_b = [T("yrd0"), T("yrd1"), T("yrd2"), T("yrd3")]
        for tq in range(NTQ):
            ln_apply_tq(tq, mr3, r3, xn3bf3, T("mr3"), T("r3"), xa_t, XT,
                        T("xn3bf"), gcol=g3c, bcol=b3c)
            if tq == 0:
                T("xn3bf").note_read((None, 0))
            for fg in range(8):
                sl = fg % 2
                for k in range(KD):
                    wr = [f1s_b[sl][k]] + ([ACCB] if tq == 0 and fg < 2 else [])
                    run("sync", lambda e, k=k, fg=fg, sl=sl: e.dma_start(
                        f1slab3s[sl][:, k, :],
                        wff1n[k * 128:(k + 1) * 128,
                              fg * 512:(fg + 1) * 512]),
                        reads=[T("wff1n")], writes=wr)
                for fl in range(4):
                    f = fg * 4 + fl
                    pb = PB[f % 2]
                    pst = PS[f % 2]
                    for k in range(KD):
                        run("tensor", lambda e, k=k, fl=fl, pst=pst, sl=sl:
                            e.matmul(
                                pst[:, 0:512],
                                f1slab3s[sl][:, k, fl * 128:(fl + 1) * 128],
                                xn3bf3[:, k, 0:512],
                                start=(k == 0), stop=(k == KD - 1)),
                            reads=[f1s_b[sl][k], T("xn3bf")], writes=[pb])
                    run("scalar", lambda e, f=f, pst=pst: e.activation(
                        h2_bf[:, f * 512:(f + 1) * 512], pst[:, 0:512],
                        AF.Gelu_apprx_tanh, bias=bff1c[:, f:f + 1]),
                        reads=[pb, T("ffcols")], writes=[h2b])
            for lt_ in range(TPQ):
                tt = tq * TPQ + lt_
                sl = lt_
                wr = [yr_b[sl]] + ([ACCB] if tq == 0 else [])
                run("sync", lambda e, tt=tt, sl=sl: e.dma_start(
                    yrd[:, sl * 1024:(sl + 1) * 1024],
                    ypart[tt * 128:(tt + 1) * 128, :]),
                    reads=[T("ypart")], writes=wr)
            for f in range(KF):
                sl = f % 4
                wv = wf2slab[:, sl * 1024:(sl + 1) * 1024]
                wr = [wf2_b[sl]] + ([ACCB] if tq == 0 and f < 4 else [])
                run("gpsimd", lambda e, w=wv, f=f: e.dma_start(
                    w, wff2n[f * 128:(f + 1) * 128, :]),
                    reads=[T("wff2n")], writes=wr)
                for lt_ in range(TPQ):
                    for dh in range(2):
                        pi = lt_ * 2 + dh
                        run("tensor", lambda e, f=f, lt_=lt_, dh=dh, pi=pi, w=wv:
                            e.matmul(PS[pi][:, 0:512],
                                     h2_bf[:, f * 512 + lt_ * 128:
                                           f * 512 + (lt_ + 1) * 128],
                                     w[:, dh * 512:(dh + 1) * 512],
                                     start=(f == 0), stop=False),
                            reads=[wf2_b[sl], h2b], writes=[PB[pi]])
            for lt_ in range(TPQ):
                for dh in range(2):
                    pi = lt_ * 2 + dh
                    run("tensor", lambda e, pi=pi, dh=dh: e.matmul(
                        PS[pi][:, 0:512], ones_row_bf,
                        bff2row[:, dh * 512:(dh + 1) * 512],
                        start=False, stop=True),
                        reads=[T("bff2row"), T("orow")], writes=[PB[pi]])
            for lt_ in range(TPQ):
                tt = tq * TPQ + lt_
                sl = lt_
                osl = tt % 2
                ov = out_tm[:, osl * 1024:(osl + 1) * 1024]
                ob = T(f"otm{osl}")
                for dh in range(2):
                    pi = lt_ * 2 + dh
                    run("vector", lambda e, pi=pi, dh=dh, tt=tt, o=ov: e.tensor_add(
                        o[:, dh * 512:(dh + 1) * 512], PS[pi][:, 0:512],
                        xa_t(tt)[:, dh * 512:(dh + 1) * 512]),
                        reads=[PB[pi], XT[tt]], writes=[ob])
                    run("vector", lambda e, dh=dh, sl=sl, o=ov: e.tensor_add(
                        o[:, dh * 512:(dh + 1) * 512],
                        o[:, dh * 512:(dh + 1) * 512],
                        yrd[:, sl * 1024 + dh * 512:sl * 1024 + (dh + 1) * 512]),
                        reads=[yr_b[sl]], writes=[ob])
                run("sync", lambda e, o=ov, tt=tt: e.dma_start(
                    y[tt * 128:(tt + 1) * 128, :], o),
                    reads=[ob], writes=[T("y")])

        for i in range(MK.NDMA):
            for q, e2 in (("SQ", "sync"), ("GQ", "gpsimd"), ("AQ", "scalar")):
                cntv = mk.cnt.get(f"{q}{i}", 0)
                if cntv > 0:
                    mk.raw_wait("sync", f"{q}{i}", cntv)
                    if e2 != "sync":
                        mk.raw_wait(e2, f"{q}{i}", cntv)
        mk.run_block(block)

    return nc


def kernel(**inputs):
    nc = bass.Bass("TRN2", target_bir_lowering=False)
    build(nc)
    CN = _consts()
    in_maps = []
    for core in range(B):
        m = {}
        for k, v in inputs.items():
            if k == "L0_attn_x":
                m[k] = np.ascontiguousarray(np.asarray(v)[core])
            else:
                m[k] = np.ascontiguousarray(np.asarray(v))
        for k, v in CN.items():
            m[k] = v
        in_maps.append(m)
    res = run_bass_kernel_spmd(nc, in_maps, core_ids=list(range(B)))
    out = np.stack([res.results[i]["y"] for i in range(B)], axis=0)
    return out
